# version 04
# baseline (speedup 1.0000x reference)
import numpy as np
import ml_dtypes
from contextlib import ExitStack

import concourse.bass as bass
import concourse.mybir as mybir
from concourse.bass_utils import run_bass_kernel_spmd

F32 = mybir.dt.float32
BF16 = mybir.dt.bfloat16
I32 = mybir.dt.int32
AF = mybir.ActivationFunctionType
ALU = mybir.AluOpType

D = 2048
T = 2048
NF = 16384
H1 = 5632
DS = 64

ENGINES = ("sync", "gpsimd", "vector", "scalar", "tensor")

MAGIC0, MAGIC1, MAGIC2 = 0x5A00111, 0x5A00222, 0x5A00333

C_ZERO, C_ONE, C_EPS, C_75, C_KEPS, C_EPS8, C_HALF, C_X = range(8)


class Prog:

    def __init__(self, nc):
        self.nc = nc
        self.ops = {e: [] for e in ENGINES}
        self.semctr = {}
        self.marks = {}
        self.autosem = {}

    def on(self, eng, fn):
        self.ops[eng].append(fn)

    def _bump(self, sem, n):
        self.semctr[sem] = self.semctr.get(sem, 0) + n
        return self.semctr[sem]

    def val(self, sem):
        return self.semctr.get(sem, 0)

    def op(self, eng, fn, sem=None, n=1):
        auto = self.autosem.get(eng) if hasattr(self, "autosem") else None
        if auto is not None:
            v = self.val(auto)
            if v > 0:
                self.on(eng, lambda E, s=auto, v=v: E.wait_ge(s, v))
            if sem is None:
                sem, n = auto, 1
            else:
                assert sem is auto, (eng, "non-default sem on auto-synced engine")
        if sem is None:
            self.on(eng, fn)
            return None
        tgt = self._bump(sem, n)
        self.on(eng, lambda E, f=fn, s=sem, n=n: f(E).then_inc(s, n))
        return tgt

    def wait(self, eng, sem, v=None):
        if v is None:
            v = self.val(sem)
        if v > 0:
            self.on(eng, lambda E, s=sem, v=v: E.wait_ge(s, v))

    def dma(self, eng, out, in_, sem, **kw):
        return self.op(eng, lambda E, o=out, i=in_, kw=kw:
                       E.dma_start(out=o, in_=i, **kw), sem, 16)

    def run(self, block):
        for e in ENGINES:
            stream = self.ops[e]
            if stream:
                getattr(block, e)(lambda E, s=stream: [f(E) for f in s])




ALL_SEMS = ["sd", "sg", "sg1", "so", "so1", "sv", "ss", "sm", "se",
            "sk0", "sk1", "sc0", "sc1", "st0", "st1", "si0", "si1",
            "sdp", "sdh", "sx", "spl0", "spl1", "spl2"]
POLL_SEMS = {"spl0", "spl1", "spl2"}


class FCtx:

    def __init__(self, nc, p, S):
        self.nc, self.p, self.S = nc, p, S
        self.io = {}


def _phase(fctx, names):
    return fctx.nc, fctx.S, fctx.p


def _io(fctx, nc, name, shape, dtype, kind):
    if fctx is None:
        return nc.dram_tensor(name, shape, dtype, kind=kind).ap()
    return fctx.io[name]


def _barrier(p, S):
    for e in ENGINES:
        for k, s in S.items():
            if k in POLL_SEMS:
                continue
            p.wait(e, s)


def _mk_nc():
    return bass.Bass("TRN2", target_bir_lowering=False, debug=False)


def _sems(es, nc, names):
    return {n: es.enter_context(nc.semaphore(n)) for n in names}



def build_ln(n_parts, out_x=False, fctx=None, pre=""):
    nc = fctx.nc if fctx else _mk_nc()
    xs_h = [_io(fctx, nc, pre + f"xp{i}", [D, T], F32, "ExternalInput")
            for i in range(n_parts)]
    consts = _io(fctx, nc, pre + "consts", [128, 8], F32, "ExternalInput")
    onesc_h = _io(fctx, nc, pre + "onesc", [128, 1], F32, "ExternalInput")
    onesr_h = _io(fctx, nc, pre + "onesr", [1, 128], F32, "ExternalInput")
    xn_h = _io(fctx, nc, pre + "xn", [D, T], BF16, "ExternalOutput")
    xsum_h = (_io(fctx, nc, pre + "xsum", [D, T], F32, "ExternalOutput")
              if out_x else None)
    es = ExitStack()
    with es:
        xs = es.enter_context(nc.sbuf_tensor(pre + "xs", [128, 16 * T], F32))
        xtmp = es.enter_context(nc.sbuf_tensor(pre + "xtmp", [128, T], F32))
        sq = es.enter_context(nc.sbuf_tensor(pre + "sq", [128, T], F32))
        ct = es.enter_context(nc.sbuf_tensor(pre + "ct", [128, 8], F32))
        onesc = es.enter_context(nc.sbuf_tensor(pre + "s_onesc", [128, 1], F32))
        onesr = es.enter_context(nc.sbuf_tensor(pre + "s_onesr", [1, 128], F32))
        srow = es.enter_context(nc.sbuf_tensor(pre + "srow", [1, 4 * T], F32))
        bc = es.enter_context(nc.sbuf_tensor(pre + "bc", [128, 2 * T], F32))
        xnb = es.enter_context(nc.sbuf_tensor(pre + "xnb", [128, 2 * T], BF16))
        ps = es.enter_context(nc.psum_tensor(pre + "ps", [1, T], F32))
        psb = es.enter_context(nc.psum_tensor(pre + "psb", [128, 512], F32))
        S = fctx.S if fctx else _sems(es, nc, ALL_SEMS)
        sd, sg, so, so1, sv, ss, sm = (S[k] for k in
                                       ["sd", "sg", "so", "so1", "sv", "ss", "sm"])
        soP = [so, so1]
        if fctx is None:
            p = Prog(nc)
            p.autosem = {"vector": sv, "scalar": ss}
        else:
            p = fctx.p

        p.dma("sync", ct[:, :], consts, sd)
        p.dma("sync", onesc[:, :], onesc_h, sd)
        p.dma("sync", onesr[:, :], onesr_h, sd)
        for c in range(16):
            p.dma("sync", xs[:, c * T:(c + 1) * T], xs_h[0][c * 128:(c + 1) * 128, :], sd)
        for ip in range(1, n_parts):
            for c in range(16):
                p.dma("sync", xtmp[:, :], xs_h[ip][c * 128:(c + 1) * 128, :], sd)
                p.wait("vector", sd)
                p.wait("vector", sg)
                p.op("vector", lambda E, c=c: E.tensor_tensor(
                    xs[:, c * T:(c + 1) * T], xs[:, c * T:(c + 1) * T],
                    xtmp[:, :], ALU.add), sv)
                p.wait("sync", sv)
        if out_x:
            p.wait("sync", sv)
            for c in range(16):
                p.dma("sync", xsum_h[c * 128:(c + 1) * 128, :],
                      xs[:, c * T:(c + 1) * T], so)
        p.wait("tensor", sd)
        p.wait("tensor", sg)
        p.wait("tensor", sv)
        for c in range(16):
            for tt in range(4):
                p.op("tensor", lambda E, c=c, tt=tt: E.matmul(
                    ps[0:1, tt * 512:(tt + 1) * 512], onesc[:, :],
                    xs[:, c * T + tt * 512:c * T + (tt + 1) * 512],
                    start=(c == 0), stop=(c == 15)),
                    sm if (c == 15 and tt == 3) else None)
        p.wait("vector", sm)
        MU, RS, NB, V1 = 0, 1, 2, 3
        p.op("vector", lambda E: E.tensor_scalar(
            srow[0:1, MU * T:(MU + 1) * T], ps[0:1, :], 1.0 / D, None, ALU.mult), sv)
        p.wait("scalar", sv)
        for c in range(16):
            p.wait("scalar", sm, p.val(sm))
            p.op("scalar", lambda E, c=c: E.activation(
                sq[:, :], xs[:, c * T:(c + 1) * T], AF.Square), ss)
            p.wait("tensor", ss)
            for tt in range(4):
                p.op("tensor", lambda E, c=c, tt=tt: E.matmul(
                    ps[0:1, tt * 512:(tt + 1) * 512], onesc[:, :],
                    sq[:, tt * 512:(tt + 1) * 512],
                    start=(c == 0), stop=(c == 15)),
                    sm if (tt == 3) else None)
        p.wait("vector", sm)
        p.op("vector", lambda E: E.tensor_scalar(
            srow[0:1, V1 * T:(V1 + 1) * T], ps[0:1, :], 1.0 / D, None, ALU.mult), sv)
        p.wait("vector", sv)
        p.op("vector", lambda E: E.tensor_tensor(
            srow[0:1, RS * T:(RS + 1) * T], srow[0:1, MU * T:(MU + 1) * T],
            srow[0:1, MU * T:(MU + 1) * T], ALU.mult), sv)
        p.wait("vector", sv)
        p.op("vector", lambda E: E.tensor_tensor(
            srow[0:1, V1 * T:(V1 + 1) * T], srow[0:1, V1 * T:(V1 + 1) * T],
            srow[0:1, RS * T:(RS + 1) * T], ALU.subtract), sv)
        p.wait("scalar", sv)
        p.op("scalar", lambda E: E.activation(
            srow[0:1, RS * T:(RS + 1) * T], srow[0:1, V1 * T:(V1 + 1) * T],
            AF.Ln, bias=ct[0:1, C_EPS:C_EPS + 1]), ss)
        p.wait("scalar", ss)
        p.op("scalar", lambda E: E.activation(
            srow[0:1, RS * T:(RS + 1) * T], srow[0:1, RS * T:(RS + 1) * T],
            AF.Exp, scale=-0.5), ss)
        p.wait("vector", ss)
        p.op("vector", lambda E: E.tensor_tensor(
            srow[0:1, NB * T:(NB + 1) * T], srow[0:1, MU * T:(MU + 1) * T],
            srow[0:1, RS * T:(RS + 1) * T], ALU.mult), sv)
        p.wait("vector", sv)
        p.op("vector", lambda E: E.tensor_scalar(
            srow[0:1, NB * T:(NB + 1) * T], srow[0:1, NB * T:(NB + 1) * T],
            -1.0, None, ALU.mult), sv)
        p.wait("vector", sv)
        p.wait("tensor", sv)
        for half, col in ((0, RS), (1, NB)):
            for tt in range(4):
                p.op("tensor", lambda E, col=col, tt=tt: E.matmul(
                    psb[:, :], onesr[:, :],
                    srow[0:1, col * T + tt * 512:col * T + (tt + 1) * 512],
                    start=True, stop=True), sm)
                p.wait("vector", sm)
                p.op("vector", lambda E, half=half, tt=tt: E.tensor_copy(
                    bc[:, half * T + tt * 512:half * T + (tt + 1) * 512],
                    psb[:, :]), sv)
                p.wait("tensor", sv)
        for c in range(16):
            buf = c % 2
            if c >= 2:
                p.wait("vector", soP[buf])
            p.op("vector", lambda E, c=c: E.tensor_tensor(
                sq[:, :], xs[:, c * T:(c + 1) * T], bc[:, 0:T], ALU.mult), sv)
            p.op("vector", lambda E, c=c, buf=buf: E.tensor_tensor(
                xnb[:, buf * T:(buf + 1) * T], sq[:, :], bc[:, T:2 * T], ALU.add), sv)
            p.wait("sync", sv)
            p.dma("sync", xn_h[c * 128:(c + 1) * 128, :],
                  xnb[:, buf * T:(buf + 1) * T], soP[buf])
        p.wait("sync", so)
        p.wait("sync", so1)
        p.wait("sync", sd)
        p.wait("sync", sg)
        if fctx is None:
            block = es.enter_context(nc.Block())
            p.run(block)
        else:
            _barrier(p, S)
    return nc



def _emit_qkv_chunk(p, tiles, ecs, blk_off, rope_pair, fuse, S):
    nc = p.nc
    sd, sg, sv, ss, sm, se = S["sd"], S["sg"], S["sv"], S["ss"], S["sm"], S["se"]
    xn, wb = tiles["xn"], tiles["wb"]
    winT = tiles["winT_h"]
    pss = [tiles["ps0"], tiles["ps1"]]
    t12 = [tiles["t1"], tiles["t2"]]
    binc = tiles["binc"]
    p.wait("gpsimd", sm)
    for j, ec in enumerate(ecs):
        col = blk_off + ec * 128
        p.dma("gpsimd", wb[:, j * 2048:(j + 1) * 2048],
              winT.rearrange("(dc k) e -> k dc e", k=128)[:, :, col:col + 128], sg)
    p.wait("tensor", sd)
    p.wait("tensor", sg)
    p.wait("tensor", sv, p.marks.get("qkv_psum_free", 0))
    for j, ec in enumerate(ecs):
        for tt in range(4):
            for dc in range(16):
                p.op("tensor", lambda E, j=j, dc=dc, tt=tt: E.matmul(
                    pss[j][:, tt * 512:(tt + 1) * 512],
                    wb[:, j * 2048 + dc * 128:j * 2048 + (dc + 1) * 128],
                    xn[:, dc * T + tt * 512:dc * T + (tt + 1) * 512],
                    start=(dc == 0), stop=(dc == 15)),
                    sm if (dc == 15 and tt == 3) else None)
    p.wait("scalar", sm)
    p.wait("scalar", sv)
    for j, ec in enumerate(ecs):
        bcol = (blk_off + ec * 128) // 128
        p.op("scalar", lambda E, j=j, bcol=bcol: E.activation(
            t12[j][:, :], pss[j][:, :], AF.Identity,
            bias=binc[:, bcol:bcol + 1]), ss)
    if rope_pair:
        cosb, sinb = tiles["cosb"], tiles["sinb"]
        r, u = tiles["rsc"], tiles["ru"]
        p.wait("vector", ss)
        p.op("vector", lambda E: E.tensor_tensor(r[:, 0:T], t12[1][:, :], sinb[:, :], ALU.mult), sv)
        p.op("vector", lambda E: E.tensor_tensor(u[:, 0:T], t12[0][:, :], sinb[:, :], ALU.mult), sv)
        p.op("vector", lambda E: E.tensor_tensor(t12[0][:, :], t12[0][:, :], cosb[:, :], ALU.mult), sv)
        p.op("vector", lambda E: E.tensor_tensor(t12[1][:, :], t12[1][:, :], cosb[:, :], ALU.mult), sv)
        p.op("vector", lambda E: E.tensor_tensor(t12[0][:, :], t12[0][:, :], r[:, 0:T], ALU.subtract), sv)
        p.op("vector", lambda E: E.tensor_tensor(t12[1][:, :], t12[1][:, :], u[:, 0:T], ALU.add), sv)
    fuse(p)
    p.marks["qkv_psum_free"] = p.val(sv)


def build_kva(fctx=None, pre=""):
    nc = fctx.nc if fctx else _mk_nc()
    xn_h = _io(fctx, nc, pre + "xn", [D, T], BF16, "ExternalInput")
    winT_h = _io(fctx, nc, pre + "winT", [D, 3 * D], F32, "ExternalInput")
    binc_h = _io(fctx, nc, pre + "binc", [128, 48], F32, "ExternalInput")
    cos_h = _io(fctx, nc, pre + "cosr", [512, T], F32, "ExternalInput")
    sin_h = _io(fctx, nc, pre + "sinr", [512, T], F32, "ExternalInput")
    consts = _io(fctx, nc, pre + "consts", [128, 8], F32, "ExternalInput")
    onesc_h = _io(fctx, nc, pre + "onesc", [128, 1], F32, "ExternalInput")
    onesr_h = _io(fctx, nc, pre + "onesr", [1, 128], F32, "ExternalInput")
    ap_h = _io(fctx, nc, pre + "apart", [128, 16], F32, "ExternalOutput")
    es = ExitStack()
    with es:
        xn = es.enter_context(nc.sbuf_tensor(pre + "s_xn", [128, 16 * T], BF16))
        kt = es.enter_context(nc.sbuf_tensor(pre + "kt", [128, 16 * T], BF16))
        wb = es.enter_context(nc.sbuf_tensor(pre + "wb", [128, 2 * 2048], BF16))
        t1 = es.enter_context(nc.sbuf_tensor(pre + "t1", [128, T], F32))
        t2 = es.enter_context(nc.sbuf_tensor(pre + "t2", [128, T], F32))
        rsc = es.enter_context(nc.sbuf_tensor(pre + "rsc", [128, T], F32))
        cosb = es.enter_context(nc.sbuf_tensor(pre + "cosb", [128, T], F32))
        sinb = es.enter_context(nc.sbuf_tensor(pre + "sinb", [128, T], F32))
        binc = es.enter_context(nc.sbuf_tensor(pre + "s_binc", [128, 48], F32))
        ct = es.enter_context(nc.sbuf_tensor(pre + "ct", [128, 8], F32))
        onesc = es.enter_context(nc.sbuf_tensor(pre + "s_onesc", [128, 1], F32))
        onesr = es.enter_context(nc.sbuf_tensor(pre + "s_onesr", [1, 128], F32))
        srow = es.enter_context(nc.sbuf_tensor(pre + "srow", [1, T], F32))
        rnb = es.enter_context(nc.sbuf_tensor(pre + "rnb", [128, T], F32))
        apt = es.enter_context(nc.sbuf_tensor(pre + "apt", [128, 16], F32))
        sq = es.enter_context(nc.sbuf_tensor(pre + "sq", [128, T], F32))
        ps0 = es.enter_context(nc.psum_tensor(pre + "ps0", [128, T], F32))
        ps1 = es.enter_context(nc.psum_tensor(pre + "ps1", [128, T], F32))
        S = fctx.S if fctx else _sems(es, nc, ALL_SEMS)
        sd, sg, so, sv, ss, sm, se = (S[k] for k in ["sd", "sg", "so", "sv", "ss", "sm", "se"])
        if fctx is None:
            p = Prog(nc)
            p.autosem = {"vector": sv, "scalar": ss}
        else:
            p = fctx.p
        tiles = dict(xn=xn, wb=wb, ps0=ps0, ps1=ps1, t1=t1, t2=t2, rsc=rsc,
                     ru=sq, cosb=cosb, sinb=sinb, binc=binc, winT_h=winT_h)

        p.dma("sync", binc[:, :], binc_h, sd)
        p.dma("sync", ct[:, :], consts, sd)
        p.dma("sync", onesc[:, :], onesc_h, sd)
        p.dma("sync", onesr[:, :], onesr_h, sd)
        for c in range(16):
            p.dma("sync", xn[:, c * T:(c + 1) * T], xn_h[c * 128:(c + 1) * 128, :], sd)
        for pg in range(4):
            p.wait("sync", sv)
            p.dma("sync", cosb[:, :], cos_h[pg * 128:(pg + 1) * 128, :], sd)
            p.dma("sync", sinb[:, :], sin_h[pg * 128:(pg + 1) * 128, :], sd)
            p.wait("vector", sd)
            p.wait("vector", sg)

            def fuse(p, pg=pg):
                p.wait("vector", S["ss"])
                p.op("vector", lambda E, pg=pg: E.tensor_copy(
                    kt[:, pg * T:(pg + 1) * T], t1[:, :]), sv, 1)
                p.op("vector", lambda E, pg=pg: E.tensor_copy(
                    kt[:, (pg + 4) * T:(pg + 5) * T], t2[:, :]), sv, 1)
            _emit_qkv_chunk(p, tiles, [pg, pg + 4], D, True, fuse, S)
        for kc in range(8, 16):
            def fuse(p, kc=kc):
                p.wait("vector", S["ss"])
                p.op("vector", lambda E, kc=kc: E.tensor_copy(
                    kt[:, kc * T:(kc + 1) * T], t1[:, :]), sv, 1)
            _emit_qkv_chunk(p, tiles, [kc], D, False, fuse, S)
        p.wait("scalar", sv)
        for c in range(16):
            p.wait("scalar", sm, p.val(sm))
            p.op("scalar", lambda E, c=c: E.activation(
                sq[:, :], kt[:, c * T:(c + 1) * T], AF.Square), ss)
            p.wait("tensor", ss)
            for tt in range(4):
                p.op("tensor", lambda E, c=c, tt=tt: E.matmul(
                    ps0[0:1, tt * 512:(tt + 1) * 512], onesc[:, :],
                    sq[:, tt * 512:(tt + 1) * 512],
                    start=(c == 0), stop=(c == 15)), sm if tt == 3 else None)
        p.wait("vector", sm)
        p.op("vector", lambda E: E.tensor_scalar(
            srow[0:1, :], ps0[0:1, :], 1e-24, None, ALU.max), sv)
        p.wait("scalar", sv)
        p.op("scalar", lambda E: E.activation(srow[0:1, :], srow[0:1, :], AF.Ln), ss)
        p.wait("scalar", ss)
        p.op("scalar", lambda E: E.activation(srow[0:1, :], srow[0:1, :], AF.Exp, scale=-0.5), ss)
        p.wait("tensor", ss)
        for tt in range(4):
            p.op("tensor", lambda E, tt=tt: E.matmul(
                ps1[:, tt * 512:(tt + 1) * 512], onesr[:, :],
                srow[0:1, tt * 512:(tt + 1) * 512], start=True, stop=True),
                sm if tt == 3 else None)
        p.wait("vector", sm)
        p.op("vector", lambda E: E.tensor_copy(rnb[:, :], ps1[:, :]), sv)
        p.marks["qkv_psum_free"] = p.val(sv)
        for ec in range(16):
            def fuse(p, ec=ec):
                p.wait("vector", S["ss"])
                p.op("vector", lambda E, ec=ec: E.tensor_tensor(
                    rsc[:, 0:T], kt[:, ec * T:(ec + 1) * T], t1[:, :], ALU.mult), sv)
                p.op("vector", lambda E: E.tensor_tensor(
                    rsc[:, 0:T], rsc[:, 0:T], rnb[:, :], ALU.mult), sv)
                p.op("vector", lambda E, ec=ec: E.reduce_sum(
                    apt[:, ec:ec + 1], rsc[:, 0:T], mybir.AxisListType.X), sv)
            _emit_qkv_chunk(p, tiles, [ec], 2 * D, False, fuse, S)
        p.wait("sync", sv)
        p.dma("sync", ap_h, apt[:, :], so)
        p.wait("sync", so)
        p.wait("sync", sd)
        p.wait("sync", sg)
        if fctx is None:
            block = es.enter_context(nc.Block())
            p.run(block)
        else:
            _barrier(p, S)
    return nc



def build_qattn(fctx=None, pre=""):
    nc = fctx.nc if fctx else _mk_nc()
    xn_h = _io(fctx, nc, pre + "xn", [D, T], BF16, "ExternalInput")
    x0_h = _io(fctx, nc, pre + "x0", [D, T], F32, "ExternalInput")
    winT_h = _io(fctx, nc, pre + "winT", [D, 3 * D], F32, "ExternalInput")
    binc_h = _io(fctx, nc, pre + "binc", [128, 48], F32, "ExternalInput")
    cos_h = _io(fctx, nc, pre + "cosr", [512, T], F32, "ExternalInput")
    sin_h = _io(fctx, nc, pre + "sinr", [512, T], F32, "ExternalInput")
    woutT_h = _io(fctx, nc, pre + "woutT", [D, D], F32, "ExternalInput")
    boutc_h = _io(fctx, nc, pre + "boutc", [128, 16], F32, "ExternalInput")
    am_h = _io(fctx, nc, pre + "amine", [128, 16], F32, "ExternalInput")
    ap2_h = _io(fctx, nc, pre + "apartner", [128, 16], F32, "ExternalInput")
    px1_h = _io(fctx, nc, pre + "px1", [D, T], F32, "ExternalOutput")
    es = ExitStack()
    with es:
        xn = es.enter_context(nc.sbuf_tensor(pre + "s_xn", [128, 16 * T], BF16))
        yt = es.enter_context(nc.sbuf_tensor(pre + "yt", [128, 16 * T], BF16))
        wb = es.enter_context(nc.sbuf_tensor(pre + "wb", [128, 2 * 2048], BF16))
        wob = es.enter_context(nc.sbuf_tensor(pre + "wob", [128, 2048], BF16))
        t1 = es.enter_context(nc.sbuf_tensor(pre + "t1", [128, T], F32))
        t2 = es.enter_context(nc.sbuf_tensor(pre + "t2", [128, T], F32))
        rsc = es.enter_context(nc.sbuf_tensor(pre + "rsc", [128, T], F32))
        ru = es.enter_context(nc.sbuf_tensor(pre + "ru", [128, T], F32))
        cosb = es.enter_context(nc.sbuf_tensor(pre + "cosb", [128, T], F32))
        sinb = es.enter_context(nc.sbuf_tensor(pre + "sinb", [128, T], F32))
        binc = es.enter_context(nc.sbuf_tensor(pre + "s_binc", [128, 48], F32))
        boutc = es.enter_context(nc.sbuf_tensor(pre + "s_boutc", [128, 16], F32))
        at_ = es.enter_context(nc.sbuf_tensor(pre + "at_", [128, 16], F32))
        ap2 = es.enter_context(nc.sbuf_tensor(pre + "ap2", [128, 16], F32))
        x0b = es.enter_context(nc.sbuf_tensor(pre + "x0b", [128, T], F32))
        pxb = es.enter_context(nc.sbuf_tensor(pre + "pxb", [128, T], F32))
        sq = es.enter_context(nc.sbuf_tensor(pre + "sq", [128, 1], F32))
        ps0 = es.enter_context(nc.psum_tensor(pre + "ps0", [128, T], F32))
        ps1 = es.enter_context(nc.psum_tensor(pre + "ps1", [128, T], F32))
        S = fctx.S if fctx else _sems(es, nc, ALL_SEMS)
        sd, sg, so, sv, ss, sm, se = (S[k] for k in ["sd", "sg", "so", "sv", "ss", "sm", "se"])
        if fctx is None:
            p = Prog(nc)
            p.autosem = {"vector": sv, "scalar": ss}
        else:
            p = fctx.p
        tiles = dict(xn=xn, wb=wb, ps0=ps0, ps1=ps1, t1=t1, t2=t2, rsc=rsc,
                     ru=ru, cosb=cosb, sinb=sinb, binc=binc, winT_h=winT_h)

        p.dma("sync", binc[:, :], binc_h, sd)
        p.dma("sync", boutc[:, :], boutc_h, sd)
        p.dma("sync", at_[:, :], am_h, sd)
        p.dma("sync", ap2[:, :], ap2_h, sd)
        for c in range(16):
            p.dma("sync", xn[:, c * T:(c + 1) * T], xn_h[c * 128:(c + 1) * 128, :], sd)
        p.wait("vector", sd)
        p.wait("vector", sg)
        p.op("vector", lambda E: E.tensor_tensor(at_[:, :], at_[:, :], ap2[:, :], ALU.add), sv)
        p.wait("vector", sv)

        def mk_fuse(chunks):
            def fuse(p):
                p.wait("vector", S["ss"])
                for j, qc in enumerate(chunks):
                    p.op("vector", lambda E, j=j, qc=qc: E.tensor_scalar(
                        yt[:, qc * T:(qc + 1) * T], (t1 if j == 0 else t2)[:, :],
                        at_[:, qc:qc + 1], None, ALU.mult), sv)
            return fuse
        for pg in range(4):
            p.wait("sync", sv)
            p.dma("sync", cosb[:, :], cos_h[pg * 128:(pg + 1) * 128, :], sd)
            p.dma("sync", sinb[:, :], sin_h[pg * 128:(pg + 1) * 128, :], sd)
            p.wait("vector", sd)
            p.wait("vector", sg)
            _emit_qkv_chunk(p, tiles, [pg, pg + 4], 0, True, mk_fuse([pg, pg + 4]), S)
        for qc in range(8, 16):
            _emit_qkv_chunk(p, tiles, [qc], 0, False, mk_fuse([qc]), S)

        for dc in range(16):
            p.wait("gpsimd", sm)
            p.dma("gpsimd", wob[:, :],
                  woutT_h.rearrange("(ec k) d -> k ec d", k=128)[:, :, dc * 128:(dc + 1) * 128], sg)
            p.wait("sync", sv, p.val(sv))
            p.dma("sync", x0b[:, :], x0_h[dc * 128:(dc + 1) * 128, :], sd)
            p.wait("tensor", sd)
            p.wait("tensor", sg)
            p.wait("tensor", ss, p.marks.get("qo_psum_free", 0))
            for tt in range(4):
                for ec in range(16):
                    p.op("tensor", lambda E, dc=dc, tt=tt, ec=ec: E.matmul(
                        ps0[:, tt * 512:(tt + 1) * 512],
                        wob[:, ec * 128:(ec + 1) * 128],
                        yt[:, ec * T + tt * 512:ec * T + (tt + 1) * 512],
                        start=(ec == 0), stop=(ec == 15)),
                        sm if (ec == 15 and tt == 3) else None)
            p.wait("scalar", sm)
            p.wait("scalar", sv)
            p.op("scalar", lambda E, dc=dc: E.activation(
                t1[:, :], ps0[:, :], AF.Identity, bias=boutc[:, dc:dc + 1]), ss)
            p.marks["qo_psum_free"] = p.val(ss)
            p.wait("vector", ss)
            p.op("vector", lambda E, dc=dc: E.tensor_tensor(
                t2[:, :], t1[:, :], xn[:, dc * T:(dc + 1) * T], ALU.add), sv)
            if dc >= 1:
                p.wait("vector", so, 16 * dc)
            p.op("vector", lambda E, dc=dc: E.tensor_tensor(
                pxb[:, :], t2[:, :], x0b[:, :], ALU.add), sv)
            p.wait("sync", sv)
            p.dma("sync", px1_h[dc * 128:(dc + 1) * 128, :], pxb[:, :], so)
        p.wait("sync", so)
        p.wait("sync", sd)
        p.wait("sync", sg)
        if fctx is None:
            block = es.enter_context(nc.Block())
            p.run(block)
        else:
            _barrier(p, S)
    return nc



def emit_decode(p, S, src, scl, out_lo, out_hi, scr, ct, half=False):
    sv, ss = S["sv"], S["ss"]
    i4l, i4h = scr["i4l"], scr["i4h"]
    sgl, sgh, q1, q2 = scr["sgl"], scr["sgh"], scr["q1"], scr["q2"]
    b = 0.5 if half else 1.0
    p.op("vector", lambda E: E.tensor_scalar(i4l, src, 15, None, ALU.bitwise_and), sv)
    p.op("vector", lambda E: E.tensor_scalar(i4h, src, 4, None, ALU.logical_shift_right), sv)
    p.op("vector", lambda E: E.tensor_scalar(sgl, i4l, 8, None, ALU.is_lt), sv)
    p.op("vector", lambda E: E.tensor_scalar(sgl, sgl, 2.0, -1.0, ALU.mult, ALU.add), sv)
    p.op("vector", lambda E: E.tensor_scalar(sgh, i4h, 8, None, ALU.is_lt), sv)
    p.op("vector", lambda E: E.tensor_scalar(sgh, sgh, 2.0, -1.0, ALU.mult, ALU.add), sv)
    for j4, sg, out in ((i4l, sgl, out_lo), (i4h, sgh, out_hi)):
        p.op("vector", lambda E, j4=j4: E.tensor_scalar(j4, j4, 7, None, ALU.bitwise_and), sv)
        p.op("vector", lambda E, j4=j4: E.tensor_scalar(q1, j4, 0.5 * b, None, ALU.mult), sv)
        p.op("vector", lambda E, j4=j4: E.tensor_scalar(q2, j4, b, -2.0 * b, ALU.mult, ALU.add), sv)
        p.op("vector", lambda E: E.tensor_tensor(q1, q1, q2, ALU.max), sv)
        p.op("vector", lambda E, j4=j4: E.tensor_scalar(q2, j4, 2.0 * b, -8.0 * b, ALU.mult, ALU.add), sv)
        p.op("vector", lambda E: E.tensor_tensor(q1, q1, q2, ALU.max), sv)
        p.op("vector", lambda E, sg=sg: E.tensor_tensor(q1, q1, sg, ALU.mult), sv)
        p.op("vector", lambda E, out=out: E.tensor_tensor(out, q1, scl, ALU.mult), sv)


def _decode_scratch(es, nc, W, tag=""):
    return dict(
        i4l=es.enter_context(nc.sbuf_tensor(f"i4l{tag}", [128, W], I32))[:, :],
        i4h=es.enter_context(nc.sbuf_tensor(f"i4h{tag}", [128, W], I32))[:, :],
        sgl=es.enter_context(nc.sbuf_tensor(f"sgl{tag}", [128, W], BF16))[:, :],
        sgh=es.enter_context(nc.sbuf_tensor(f"sgh{tag}", [128, W], BF16))[:, :],
        q1=es.enter_context(nc.sbuf_tensor(f"q1{tag}", [128, W], BF16))[:, :],
        q2=es.enter_context(nc.sbuf_tensor(f"q2{tag}", [128, W], BF16))[:, :],
    )


def _scr_view(scr, W):
    return {k: v[:, 0:W] for k, v in scr.items()}



def build_ffn0a(fctx=None, pre=""):
    nc = fctx.nc if fctx else _mk_nc()
    xn_h = _io(fctx, nc, pre + "xn", [D, T], BF16, "ExternalInput")
    gpk_h = _io(fctx, nc, pre + "gpk", [1024, NF], I32, "ExternalInput")
    gsc_h = _io(fctx, nc, pre + "gsc", [64, NF], F32, "ExternalInput")
    upk_h = _io(fctx, nc, pre + "upk", [1024, NF], I32, "ExternalInput")
    usc_h = _io(fctx, nc, pre + "usc", [64, NF], F32, "ExternalInput")
    dpk_h = _io(fctx, nc, pre + "dpk", [NF // 2, D], I32, "ExternalInput")
    dsc_h = _io(fctx, nc, pre + "dsc", [512, D], F32, "ExternalInput")
    consts = _io(fctx, nc, pre + "consts", [128, 8], F32, "ExternalInput")
    ddec_h = _io(fctx, nc, pre + "ddec", [NF, D], BF16, "ExternalOutput")
    inter_h = _io(fctx, nc, pre + "inter", [NF, T], BF16, "ExternalOutput")
    es = ExitStack()
    with es:
        xn = es.enter_context(nc.sbuf_tensor(pre + "s_xn", [128, 16 * T], BF16))
        ct = es.enter_context(nc.sbuf_tensor(pre + "ct", [128, 8], F32))
        gpkb = es.enter_context(nc.sbuf_tensor(pre + "gpkb", [128, 4096], I32))
        upkb = es.enter_context(nc.sbuf_tensor(pre + "upkb", [128, 4096], I32))
        gscb = es.enter_context(nc.sbuf_tensor(pre + "gscb", [128, 4096], BF16))
        uscb = es.enter_context(nc.sbuf_tensor(pre + "uscb", [128, 4096], BF16))
        glo = es.enter_context(nc.sbuf_tensor(pre + "glo", [128, 4096], BF16))
        ghi = es.enter_context(nc.sbuf_tensor(pre + "ghi", [128, 4096], BF16))
        ulo = es.enter_context(nc.sbuf_tensor(pre + "ulo", [128, 4096], BF16))
        uhi = es.enter_context(nc.sbuf_tensor(pre + "uhi", [128, 4096], BF16))
        scr = _decode_scratch(es, nc, 2048, pre)
        eg = es.enter_context(nc.sbuf_tensor(pre + "eg", [128, 512], F32))
        e2 = es.enter_context(nc.sbuf_tensor(pre + "e2", [128, 512], F32))
        g3 = es.enter_context(nc.sbuf_tensor(pre + "g3", [128, 512], F32))
        th = es.enter_context(nc.sbuf_tensor(pre + "th", [128, 512], F32))
        tp1 = es.enter_context(nc.sbuf_tensor(pre + "tp1", [128, 512], F32))
        ue = es.enter_context(nc.sbuf_tensor(pre + "ue", [128, 512], BF16))
        intb = es.enter_context(nc.sbuf_tensor(pre + "intb", [128, 2 * T], BF16))
        psg = es.enter_context(nc.psum_tensor(pre + "psg", [128, T], F32))
        psu = es.enter_context(nc.psum_tensor(pre + "psu", [128, T], F32))
        S = fctx.S if fctx else _sems(es, nc, ALL_SEMS)
        (sd, sg, so, sv, ss, sm, se, sk0, sk1, sc0, sc1, st0, st1, si0, si1) = (
            S[k] for k in ["sd", "sg", "so", "sv", "ss", "sm", "se",
                           "sk0", "sk1", "sc0", "sc1", "st0", "st1", "si0", "si1"])
        skP, scP, stP, siP = [sk0, sk1], [sc0, sc1], [st0, st1], [si0, si1]
        if fctx is None:
            p = Prog(nc)
            p.autosem = {"vector": sv, "scalar": ss}
        else:
            p = fctx.p

        p.dma("sync", ct[:, :], consts, sd)
        for c in range(16):
            p.dma("sync", xn[:, c * T:(c + 1) * T], xn_h[c * 128:(c + 1) * 128, :], sd)

        pkb, sclb, olo, ohi = gpkb, gscb, glo, ghi
        for c in range(64):
            hb = c % 2
            if c >= 2:
                p.wait("sync", sv, p.val(sv))
                p.wait("gpsimd", sv, p.val(sv))
            p.dma("sync", pkb[:, hb * D:(hb + 1) * D],
                  dpk_h[c * 128:(c + 1) * 128, :], skP[hb])
            p.dma("gpsimd", sclb[:, hb * D:(hb + 1) * D],
                  dsc_h[c * 8:(c + 1) * 8, :].unsqueeze(1).broadcast_to([8, 16, D]), scP[hb])
            p.wait("vector", sd)
            p.wait("vector", skP[hb])
            p.wait("vector", scP[hb])
            if c >= 2:
                p.wait("vector", stP[hb])
            emit_decode(p, S, pkb[:, hb * D:(hb + 1) * D], sclb[:, hb * D:(hb + 1) * D],
                        olo[:, hb * D:(hb + 1) * D], ohi[:, hb * D:(hb + 1) * D],
                        scr, ct, half=True)
            p.wait("sync", sv)
            p.dma("sync", ddec_h[c * 128:(c + 1) * 128, :], olo[:, hb * D:(hb + 1) * D], stP[hb])
            p.dma("sync", ddec_h[NF // 2 + c * 128:NF // 2 + (c + 1) * 128, :],
                  ohi[:, hb * D:(hb + 1) * D], stP[hb])

        p.wait("sync", sv)
        p.wait("gpsimd", sv)
        p.wait("vector", st0)
        p.wait("vector", st1)
        for ns in range(32):
            n0 = ns * 512
            p.wait("sync", sm)
            p.wait("gpsimd", sm)
            p.dma("sync", gpkb[:, :],
                  gpk_h.rearrange("(kc k) n -> k kc n", k=128)[:, :, n0:n0 + 512], sd)
            p.dma("sync", upkb[:, :],
                  upk_h.rearrange("(kc k) n -> k kc n", k=128)[:, :, n0:n0 + 512], sd)
            for kc in range(8):
                p.dma("gpsimd", gscb[:, kc * 512:(kc + 1) * 512],
                      gsc_h[kc * 8:(kc + 1) * 8, n0:n0 + 512]
                      .unsqueeze(1).broadcast_to([8, 16, 512]), sg)
                p.dma("gpsimd", uscb[:, kc * 512:(kc + 1) * 512],
                      usc_h[kc * 8:(kc + 1) * 8, n0:n0 + 512]
                      .unsqueeze(1).broadcast_to([8, 16, 512]), sg)
            p.wait("vector", sd)
            p.wait("vector", sg)
            p.wait("vector", sm)
            for half_ in range(2):
                W0 = half_ * 2048
                emit_decode(p, S, gpkb[:, W0:W0 + 2048], gscb[:, W0:W0 + 2048],
                            glo[:, W0:W0 + 2048], ghi[:, W0:W0 + 2048], scr, ct)
                emit_decode(p, S, upkb[:, W0:W0 + 2048], uscb[:, W0:W0 + 2048],
                            ulo[:, W0:W0 + 2048], uhi[:, W0:W0 + 2048], scr, ct)
            p.wait("tensor", sv)
            for nsub in range(4):
                nc0 = nsub * 128
                p.wait("tensor", sv, p.marks.get("gu_psum_free", 0))
                for ps_, lo, hi in ((psg, glo, ghi), (psu, ulo, uhi)):
                    for tt in range(4):
                        for kc in range(8):
                            p.op("tensor", lambda E, ps_=ps_, lo=lo, tt=tt, kc=kc, nc0=nc0: E.matmul(
                                ps_[:, tt * 512:(tt + 1) * 512],
                                lo[:, kc * 512 + nc0:kc * 512 + nc0 + 128],
                                xn[:, kc * T + tt * 512:kc * T + (tt + 1) * 512],
                                start=(kc == 0), stop=False))
                        for kc in range(8):
                            p.op("tensor", lambda E, ps_=ps_, hi=hi, tt=tt, kc=kc, nc0=nc0: E.matmul(
                                ps_[:, tt * 512:(tt + 1) * 512],
                                hi[:, kc * 512 + nc0:kc * 512 + nc0 + 128],
                                xn[:, (8 + kc) * T + tt * 512:(8 + kc) * T + (tt + 1) * 512],
                                start=False, stop=(kc == 7)),
                                sm if (kc == 7) else None)
                i = ns * 4 + nsub
                ib = i % 2
                p.wait("scalar", sm)
                if i >= 2:
                    p.wait("vector", siP[ib])
                for tt in range(4):
                    p.wait("scalar", sv, p.val(sv))
                    p.op("scalar", lambda E, tt=tt: E.activation(
                        eg[:, :], psg[:, tt * 512:(tt + 1) * 512], AF.Copy), ss)
                    p.op("scalar", lambda E, tt=tt: E.activation(
                        e2[:, :], psg[:, tt * 512:(tt + 1) * 512], AF.Square), ss)
                    p.wait("vector", ss)
                    p.op("vector", lambda E: E.tensor_tensor(g3[:, :], e2[:, :], eg[:, :], ALU.mult), sv)
                    p.op("vector", lambda E: E.scalar_tensor_tensor(
                        g3[:, :], g3[:, :], 0.044715, eg[:, :], ALU.mult, ALU.add), sv)
                    p.wait("scalar", sv)
                    p.op("scalar", lambda E: E.activation(
                        th[:, :], g3[:, :], AF.Tanh, scale=0.7978845608028654), ss)
                    p.op("scalar", lambda E: E.activation(
                        tp1[:, :], th[:, :], AF.Identity, bias=ct[:, C_ONE:C_ONE + 1]), ss)
                    p.op("scalar", lambda E, tt=tt: E.activation(
                        ue[:, :], psu[:, tt * 512:(tt + 1) * 512], AF.Copy), ss)
                    p.wait("vector", ss)
                    p.op("vector", lambda E: E.tensor_tensor(g3[:, :], eg[:, :], tp1[:, :], ALU.mult), sv)
                    p.op("vector", lambda E, ib=ib, tt=tt: E.tensor_tensor(
                        intb[:, ib * T + tt * 512:ib * T + (tt + 1) * 512],
                        g3[:, :], ue[:, :], ALU.mult), sv)
                p.marks["gu_psum_free"] = p.val(sv)
                p.wait("sync", sv)
                p.dma("sync", inter_h[n0 + nc0:n0 + nc0 + 128, :],
                      intb[:, ib * T:(ib + 1) * T], siP[ib])
        for s_ in (so, sk0, sk1, sc0, sc1, st0, st1, si0, si1, sd, sg):
            p.wait("sync", s_)
        p.wait("sync", sg)
        if fctx is None:
            block = es.enter_context(nc.Block())
            p.run(block)
        else:
            _barrier(p, S)
    return nc



def build_ffn0b(fctx=None, pre=""):
    nc = fctx.nc if fctx else _mk_nc()
    inter_h = _io(fctx, nc, pre + "inter", [NF, T], BF16, "ExternalInput")
    ddec_h = _io(fctx, nc, pre + "ddec", [NF, D], BF16, "ExternalInput")
    px1_h = _io(fctx, nc, pre + "px1", [D, T], F32, "ExternalInput")
    x1_h = _io(fctx, nc, pre + "x1", [D, T], F32, "ExternalOutput")
    es = ExitStack()
    with es:
        intt = es.enter_context(nc.sbuf_tensor(pre + "intt", [128, 128 * 512], BF16))
        ddb = es.enter_context(nc.sbuf_tensor(pre + "ddb", [128, 2 * 1024], BF16))
        pxb = es.enter_context(nc.sbuf_tensor(pre + "pxb", [128, 512], F32))
        t1 = es.enter_context(nc.sbuf_tensor(pre + "t1", [128, 512], F32))
        x1b = es.enter_context(nc.sbuf_tensor(pre + "x1b", [128, 2 * 512], F32))
        ps0 = es.enter_context(nc.psum_tensor(pre + "ps0", [128, T], F32))
        ps1 = es.enter_context(nc.psum_tensor(pre + "ps1", [128, T], F32))
        S = fctx.S if fctx else _sems(es, nc, ALL_SEMS)
        (sd, sg, sg1, so, so1, sdp, sv, ss, sm, se) = (
            S[k] for k in ["sd", "sg", "sg1", "so", "so1", "sdp", "sv", "ss", "sm", "se"])
        sgP, soP = [sg, sg1], [so, so1]
        if fctx is None:
            p = Prog(nc)
            p.autosem = {"vector": sv, "scalar": ss}
        else:
            p = fctx.p

        def psv(hc8):
            ps_ = ps0 if hc8 < 4 else ps1
            return ps_[:, (hc8 % 4) * 512:(hc8 % 4 + 1) * 512]

        for tt in range(4):
            p.wait("sync", sm)
            for half in range(2):
                p.dma("sync", intt[:, half * 64 * 512:(half + 1) * 64 * 512],
                      inter_h.rearrange("(ncc k) t -> k ncc t", k=128)
                      [:, half * 64:(half + 1) * 64, tt * 512:(tt + 1) * 512], sd)
            for hg in range(2):
                p.wait("tensor", ss, p.marks.get("d2_psum_free", 0))
                p.wait("tensor", sd)
                for ncc in range(128):
                    pb = ncc % 2
                    p.wait("gpsimd", sm, p.val(sm) - (1 if ncc >= 2 else 0))
                    p.dma("gpsimd", ddb[:, pb * 1024:(pb + 1) * 1024],
                          ddec_h[ncc * 128:(ncc + 1) * 128,
                                 hg * 1024:(hg + 1) * 1024], sgP[pb])
                    p.wait("tensor", sgP[pb])
                    for hc8 in range(8):
                        p.op("tensor", lambda E, pb=pb, hc8=hc8, ncc=ncc: E.matmul(
                            psv(hc8), ddb[:, pb * 1024 + hc8 * 128:pb * 1024 + (hc8 + 1) * 128],
                            intt[:, ncc * 512:(ncc + 1) * 512],
                            start=(ncc == 0), stop=(ncc == 127)),
                            sm if hc8 == 7 else None)
                p.wait("scalar", sm)
                for hc8 in range(8):
                    hc = hg * 8 + hc8
                    i = tt * 16 + hc
                    p.wait("sync", sv, p.val(sv))
                    p.dma("sync", pxb[:, :],
                          px1_h[hc * 128:(hc + 1) * 128, tt * 512:(tt + 1) * 512], sdp)
                    p.wait("scalar", sv, p.val(sv))
                    p.op("scalar", lambda E, hc8=hc8: E.activation(
                        t1[:, :], psv(hc8), AF.Copy), ss)
                    if hc8 == 7:
                        p.marks["d2_psum_free"] = p.val(ss)
                    p.wait("vector", ss)
                    p.wait("vector", sdp)
                    b = i % 2
                    if i >= 2:
                        p.wait("vector", soP[b])
                    p.op("vector", lambda E, b=b: E.tensor_tensor(
                        x1b[:, b * 512:(b + 1) * 512], t1[:, :], pxb[:, :], ALU.add), sv)
                    p.wait("sync", sv)
                    p.dma("sync", x1_h[hc * 128:(hc + 1) * 128, tt * 512:(tt + 1) * 512],
                          x1b[:, b * 512:(b + 1) * 512], soP[b])
        for s_ in (so, so1, sd, sg, sg1, sdp):
            p.wait("sync", s_)
        if fctx is None:
            block = es.enter_context(nc.Block())
            p.run(block)
        else:
            _barrier(p, S)
    return nc



def build_zmm(fctx=None, pre=""):
    nc = fctx.nc if fctx else _mk_nc()
    xn_h = _io(fctx, nc, pre + "xn2", [D, T], BF16, "ExternalInput")
    wsh_h = _io(fctx, nc, pre + "wshT", [D, D], F32, "ExternalInput")
    z_h = _io(fctx, nc, pre + "z", [D, T], BF16, "ExternalOutput")
    es = ExitStack()
    with es:
        xn = es.enter_context(nc.sbuf_tensor(pre + "s_xn", [128, 16 * T], BF16))
        wb = es.enter_context(nc.sbuf_tensor(pre + "wb", [128, 2 * 2048], BF16))
        zb = es.enter_context(nc.sbuf_tensor(pre + "zb", [128, 2 * T], BF16))
        ps0 = es.enter_context(nc.psum_tensor(pre + "ps0", [128, T], F32))
        ps1 = es.enter_context(nc.psum_tensor(pre + "ps1", [128, T], F32))
        S = fctx.S if fctx else _sems(es, nc, ALL_SEMS)
        (sd, sg, sg1, so, so1, sv, ss, sm, se) = (
            S[k] for k in ["sd", "sg", "sg1", "so", "so1", "sv", "ss", "sm", "se"])
        sgP, soP = [sg, sg1], [so, so1]
        if fctx is None:
            p = Prog(nc)
            p.autosem = {"vector": sv, "scalar": ss}
        else:
            p = fctx.p
        for c in range(16):
            p.dma("sync", xn[:, c * T:(c + 1) * T], xn_h[c * 128:(c + 1) * 128, :], sd)
        for ec in range(16):
            eb = ec % 2
            ps_ = ps0 if eb == 0 else ps1
            p.wait("gpsimd", sm, max(0, p.val(sm) - 1))
            p.dma("gpsimd", wb[:, eb * 2048:(eb + 1) * 2048],
                  wsh_h.rearrange("(dc k) e -> k dc e", k=128)[:, :, ec * 128:(ec + 1) * 128], sgP[eb])
            p.wait("tensor", sd)
            p.wait("tensor", sgP[eb])
            if ec >= 2:
                p.wait("tensor", sv, p.marks.get(f"z_psum_free_{ec % 2}", 0))
            for tt in range(4):
                for dc in range(16):
                    p.op("tensor", lambda E, ps_=ps_, eb=eb, tt=tt, dc=dc: E.matmul(
                        ps_[:, tt * 512:(tt + 1) * 512],
                        wb[:, eb * 2048 + dc * 128:eb * 2048 + (dc + 1) * 128],
                        xn[:, dc * T + tt * 512:dc * T + (tt + 1) * 512],
                        start=(dc == 0), stop=(dc == 15)),
                        sm if (dc == 15 and tt == 3) else None)
            p.wait("vector", sm)
            if ec >= 2:
                p.wait("vector", soP[eb])
            p.op("vector", lambda E, ps_=ps_, eb=eb: E.tensor_copy(
                zb[:, eb * T:(eb + 1) * T], ps_[:, :]), sv)
            p.marks[f"z_psum_free_{eb}"] = p.val(sv)
            p.wait("sync", sv)
            p.dma("sync", z_h[ec * 128:(ec + 1) * 128, :], zb[:, eb * T:(eb + 1) * T], soP[eb])
        for s_ in (so, so1, sd, sg, sg1):
            p.wait("sync", s_)
        p.wait("sync", sg)
        if fctx is None:
            block = es.enter_context(nc.Block())
            p.run(block)
        else:
            _barrier(p, S)
    return nc



def build_l1attn(fctx=None, pre=""):
    nc = fctx.nc if fctx else _mk_nc()
    xn_h = _io(fctx, nc, pre + "xn2", [D, T], BF16, "ExternalInput")
    zp_h = _io(fctx, nc, pre + "zp", [D, T], BF16, "ExternalInput")
    gatec_h = _io(fctx, nc, pre + "gatec", [128, 16], F32, "ExternalInput")
    lngc_h = _io(fctx, nc, pre + "lngc", [128, 16], F32, "ExternalInput")
    lnbc_h = _io(fctx, nc, pre + "lnbc", [128, 16], F32, "ExternalInput")
    wkvT_h = _io(fctx, nc, pre + "wkvT", [D, 128], F32, "ExternalInput")
    tdc_h = _io(fctx, nc, pre + "tdc", [64, 1], F32, "ExternalInput")
    tfc_h = _io(fctx, nc, pre + "tfc", [64, 1], F32, "ExternalInput")
    consts = _io(fctx, nc, pre + "consts", [128, 8], F32, "ExternalInput")
    onesc_h = _io(fctx, nc, pre + "onesc", [128, 1], F32, "ExternalInput")
    onescb_h = _io(fctx, nc, pre + "onescb", [128, 1], BF16, "ExternalInput")
    onesr_h = _io(fctx, nc, pre + "onesr", [1, 128], F32, "ExternalInput")
    st_h = _io(fctx, nc, pre + "states", [64, T], F32, "ExternalOutput")
    sf_h = _io(fctx, nc, pre + "sfin", [64, 1], F32, "ExternalOutput")
    es = ExitStack()
    with es:
        xst = es.enter_context(nc.sbuf_tensor(pre + "xst", [128, 16 * T], BF16))
        xb = es.enter_context(nc.sbuf_tensor(pre + "xb", [128, T], BF16))
        zb = es.enter_context(nc.sbuf_tensor(pre + "zb", [128, T], BF16))
        t1 = es.enter_context(nc.sbuf_tensor(pre + "t1", [128, T], F32))
        sq = es.enter_context(nc.sbuf_tensor(pre + "sq", [128, T], F32))
        gt = es.enter_context(nc.sbuf_tensor(pre + "gt", [128, 16], F32))
        gmt = es.enter_context(nc.sbuf_tensor(pre + "gmt", [128, 16], F32))
        lng = es.enter_context(nc.sbuf_tensor(pre + "lng", [128, 16], F32))
        lnb = es.enter_context(nc.sbuf_tensor(pre + "lnb", [128, 16], F32))
        ct = es.enter_context(nc.sbuf_tensor(pre + "ct", [128, 8], F32))
        onesc = es.enter_context(nc.sbuf_tensor(pre + "s_onesc", [128, 1], F32))
        onescb = es.enter_context(nc.sbuf_tensor(pre + "s_onescb", [128, 1], BF16))
        onesr = es.enter_context(nc.sbuf_tensor(pre + "s_onesr", [1, 128], F32))
        srow = es.enter_context(nc.sbuf_tensor(pre + "srow", [1, 4 * T], F32))
        bc = es.enter_context(nc.sbuf_tensor(pre + "bc", [128, 2 * T], F32))
        wkv_w = es.enter_context(nc.sbuf_tensor(pre + "wkv_w", [128, 16 * 128], BF16))
        xnrm = es.enter_context(nc.sbuf_tensor(pre + "xnrm", [128, 16 * 512], BF16))
        kwk = es.enter_context(nc.sbuf_tensor(pre + "kwk", [64, 3 * 512], F32))
        wkvt = es.enter_context(nc.sbuf_tensor(pre + "wkvt", [64, T], F32))
        stt = es.enter_context(nc.sbuf_tensor(pre + "s_stt", [64, T], F32))
        tdc = es.enter_context(nc.sbuf_tensor(pre + "s_tdc", [64, 1], F32))
        negu = es.enter_context(nc.sbuf_tensor(pre + "negu", [64, 1], F32))
        wcol = es.enter_context(nc.sbuf_tensor(pre + "wcol", [64, 1], F32))
        ps = es.enter_context(nc.psum_tensor(pre + "ps", [1, T], F32))
        psb = es.enter_context(nc.psum_tensor(pre + "psb", [128, 512], F32))
        pskv = es.enter_context(nc.psum_tensor(pre + "pskv", [128, 512], F32))
        S = fctx.S if fctx else _sems(es, nc, ALL_SEMS)
        sd, sg, so, sv, ss, sm, se = (S[k] for k in ["sd", "sg", "so", "sv", "ss", "sm", "se"])
        if fctx is None:
            p = Prog(nc)
            p.autosem = {"vector": sv, "scalar": ss}
        else:
            p = fctx.p

        p.dma("sync", ct[:, :], consts, sd)
        p.dma("sync", gt[:, :], gatec_h, sd)
        p.dma("sync", lng[:, :], lngc_h, sd)
        p.dma("sync", lnb[:, :], lnbc_h, sd)
        p.dma("sync", onesc[:, :], onesc_h, sd)
        p.dma("sync", onescb[:, :], onescb_h, sd)
        p.dma("sync", onesr[:, :], onesr_h, sd)
        p.dma("sync", tdc[:, :], tdc_h, sd)
        p.dma("sync", negu[:, :], tfc_h, sd)
        p.dma("gpsimd", wkv_w[:, :],
              wkvT_h.rearrange("(dc k) s -> k dc s", k=128), sg)
        p.wait("scalar", sd)
        p.wait("scalar", sg)
        p.op("scalar", lambda E: E.activation(gt[:, :], gt[:, :], AF.Sigmoid), ss)
        p.op("scalar", lambda E: E.activation(negu[:, :], negu[:, :], AF.Exp), ss)
        p.op("scalar", lambda E: E.activation(wcol[:, :], tdc[:, :], AF.Exp), ss)
        p.wait("vector", ss)
        p.op("vector", lambda E: E.tensor_scalar(gmt[:, :], gt[:, :], -1.0, 1.0, ALU.mult, ALU.add), sv)
        p.op("vector", lambda E: E.tensor_scalar(negu[:, :], negu[:, :], -1.0, None, ALU.mult), sv)
        p.wait("vector", sv)

        for c in range(16):
            p.wait("sync", sv, p.val(sv))
            p.dma("sync", xb[:, :], xn_h[c * 128:(c + 1) * 128, :], sd)
            p.dma("sync", zb[:, :], zp_h[c * 128:(c + 1) * 128, :], sd)
            p.wait("vector", sd)
            p.wait("vector", sg)
            p.op("vector", lambda E, c=c: E.tensor_scalar(
                sq[:, :], zb[:, :], gt[:, c:c + 1], None, ALU.mult), sv)
            p.op("vector", lambda E, c=c: E.tensor_scalar(
                t1[:, :], xb[:, :], gmt[:, c:c + 1], None, ALU.mult), sv)
            p.op("vector", lambda E, c=c: E.tensor_tensor(
                xst[:, c * T:(c + 1) * T], sq[:, :], t1[:, :], ALU.add), sv)
        p.wait("tensor", sv)
        for c in range(16):
            for tt in range(4):
                p.op("tensor", lambda E, c=c, tt=tt: E.matmul(
                    ps[0:1, tt * 512:(tt + 1) * 512], onescb[:, :],
                    xst[:, c * T + tt * 512:c * T + (tt + 1) * 512],
                    start=(c == 0), stop=(c == 15)),
                    sm if (c == 15 and tt == 3) else None)
        p.wait("vector", sm)
        MU, RS, NB, V1 = 0, 1, 2, 3
        p.op("vector", lambda E: E.tensor_scalar(
            srow[0:1, MU * T:(MU + 1) * T], ps[0:1, :], 1.0 / D, None, ALU.mult), sv)
        p.wait("scalar", sv)
        for c in range(16):
            p.wait("scalar", sm, p.val(sm))
            p.op("scalar", lambda E, c=c: E.activation(
                sq[:, :], xst[:, c * T:(c + 1) * T], AF.Square), ss)
            p.wait("tensor", ss)
            for tt in range(4):
                p.op("tensor", lambda E, c=c, tt=tt: E.matmul(
                    ps[0:1, tt * 512:(tt + 1) * 512], onesc[:, :],
                    sq[:, tt * 512:(tt + 1) * 512],
                    start=(c == 0), stop=(c == 15)),
                    sm if (tt == 3) else None)
        p.wait("vector", sm)
        p.op("vector", lambda E: E.tensor_scalar(
            srow[0:1, V1 * T:(V1 + 1) * T], ps[0:1, :], 1.0 / D, None, ALU.mult), sv)
        p.wait("vector", sv)
        p.op("vector", lambda E: E.tensor_tensor(
            srow[0:1, RS * T:(RS + 1) * T], srow[0:1, MU * T:(MU + 1) * T],
            srow[0:1, MU * T:(MU + 1) * T], ALU.mult), sv)
        p.wait("vector", sv)
        p.op("vector", lambda E: E.tensor_tensor(
            srow[0:1, V1 * T:(V1 + 1) * T], srow[0:1, V1 * T:(V1 + 1) * T],
            srow[0:1, RS * T:(RS + 1) * T], ALU.subtract), sv)
        p.wait("scalar", sv)
        p.op("scalar", lambda E: E.activation(
            srow[0:1, RS * T:(RS + 1) * T], srow[0:1, V1 * T:(V1 + 1) * T],
            AF.Ln, bias=ct[0:1, C_EPS:C_EPS + 1]), ss)
        p.wait("scalar", ss)
        p.op("scalar", lambda E: E.activation(
            srow[0:1, RS * T:(RS + 1) * T], srow[0:1, RS * T:(RS + 1) * T],
            AF.Exp, scale=-0.5), ss)
        p.wait("vector", ss)
        p.op("vector", lambda E: E.tensor_tensor(
            srow[0:1, NB * T:(NB + 1) * T], srow[0:1, MU * T:(MU + 1) * T],
            srow[0:1, RS * T:(RS + 1) * T], ALU.mult), sv)
        p.wait("vector", sv)
        p.op("vector", lambda E: E.tensor_scalar(
            srow[0:1, NB * T:(NB + 1) * T], srow[0:1, NB * T:(NB + 1) * T],
            -1.0, None, ALU.mult), sv)
        p.wait("vector", sv)
        p.wait("tensor", sv)
        for half, col in ((0, RS), (1, NB)):
            for tt in range(4):
                p.op("tensor", lambda E, col=col, tt=tt: E.matmul(
                    psb[:, :], onesr[:, :],
                    srow[0:1, col * T + tt * 512:col * T + (tt + 1) * 512],
                    start=True, stop=True), sm)
                p.wait("vector", sm)
                p.op("vector", lambda E, half=half, tt=tt: E.tensor_copy(
                    bc[:, half * T + tt * 512:half * T + (tt + 1) * 512],
                    psb[:, :]), sv)
                p.wait("tensor", sv)
        for tt in range(4):
            p.wait("vector", sm)
            for dc in range(16):
                p.op("vector", lambda E, tt=tt, dc=dc: E.tensor_tensor(
                    sq[:, 0:512], xst[:, dc * T + tt * 512:dc * T + (tt + 1) * 512],
                    bc[:, tt * 512:(tt + 1) * 512], ALU.mult), sv)
                p.op("vector", lambda E, tt=tt, dc=dc: E.tensor_tensor(
                    sq[:, 0:512], sq[:, 0:512],
                    bc[:, T + tt * 512:T + (tt + 1) * 512], ALU.add), sv)
                p.op("vector", lambda E, dc=dc: E.tensor_scalar(
                    xnrm[:, dc * 512:(dc + 1) * 512], sq[:, 0:512],
                    lng[:, dc:dc + 1], lnb[:, dc:dc + 1], ALU.mult, ALU.add), sv)
            p.wait("tensor", sv)
            p.wait("tensor", sg)
            p.wait("tensor", ss)
            for dc in range(16):
                p.op("tensor", lambda E, dc=dc: E.matmul(
                    pskv[:, :], wkv_w[:, dc * 128:(dc + 1) * 128],
                    xnrm[:, dc * 512:(dc + 1) * 512],
                    start=(dc == 0), stop=(dc == 15)),
                    sm if dc == 15 else None)
            p.wait("scalar", sm)
            p.op("scalar", lambda E: E.activation(kwk[0:64, 0:512], pskv[0:64, :], AF.Sigmoid), ss)
            p.wait("vector", ss)
            p.op("vector", lambda E: E.tensor_scalar(
                kwk[0:64, 512:1024], kwk[0:64, 0:512], negu[:, 0:1], None, ALU.mult), sv)
            p.wait("scalar", sv)
            p.op("scalar", lambda E: E.activation(kwk[0:64, 1024:1536], kwk[0:64, 512:1024], AF.Exp), ss)
            p.wait("vector", ss)
            p.op("vector", lambda E, tt=tt: E.tensor_tensor(
                wkvt[0:64, tt * 512:(tt + 1) * 512], kwk[0:64, 1024:1536],
                pskv[64:128, :], ALU.mult), sv, 1)
        p.wait("vector", sv)
        p.op("vector", lambda E: E.tensor_tensor_scan(
            stt[0:64, :], wcol[0:64, 0:1].broadcast_to([64, T]), wkvt[0:64, :],
            ct[0:64, C_ZERO:C_ZERO + 1], ALU.mult, ALU.add), sv)
        p.wait("sync", sv)
        p.dma("sync", st_h, stt[0:64, :], so)
        p.dma("sync", sf_h, stt[0:64, T - 1:T], so)
        p.wait("sync", so)
        p.wait("sync", sd)
        p.wait("sync", sg)
        if fctx is None:
            block = es.enter_context(nc.Block())
            p.run(block)
        else:
            _barrier(p, S)
    return nc



def build_l1proj(fctx=None, pre=""):
    nc = fctx.nc if fctx else _mk_nc()
    st_in = _io(fctx, nc, pre + "states", [64, T], F32, "ExternalInput")
    sp_h = _io(fctx, nc, pre + "sp", [64, 1], F32, "ExternalInput")
    sz_h = _io(fctx, nc, pre + "szero", [64, 1], F32, "ExternalInput")
    tdr_h = _io(fctx, nc, pre + "tdrow", [1, 64], F32, "ExternalInput")
    pos_h = _io(fctx, nc, pre + "posr", [1, T], F32, "ExternalInput")
    wop_h = _io(fctx, nc, pre + "woutpT", [64, D], F32, "ExternalInput")
    at_h = _io(fctx, nc, pre + "attn1", [D, T], F32, "ExternalOutput")
    es = ExitStack()
    with es:
        stt = es.enter_context(nc.sbuf_tensor(pre + "s_stt", [64, T], F32))
        stb = es.enter_context(nc.sbuf_tensor(pre + "stb", [64, T], BF16))
        wp = es.enter_context(nc.sbuf_tensor(pre + "wp", [64, T], F32))
        spc = es.enter_context(nc.sbuf_tensor(pre + "spc", [64, 1], F32))
        szc = es.enter_context(nc.sbuf_tensor(pre + "szc", [64, 1], F32))
        tdr = es.enter_context(nc.sbuf_tensor(pre + "tdr", [1, 64], F32))
        posr = es.enter_context(nc.sbuf_tensor(pre + "s_posr", [1, T], F32))
        wob = es.enter_context(nc.sbuf_tensor(pre + "wob", [64, D], BF16))
        ob = es.enter_context(nc.sbuf_tensor(pre + "ob", [128, 2 * T], F32))
        psw = es.enter_context(nc.psum_tensor(pre + "psw", [64, T], F32))
        pso = es.enter_context(nc.psum_tensor(pre + "pso", [128, T], F32))
        S = fctx.S if fctx else _sems(es, nc, ALL_SEMS)
        (sd, sg, so, so1, sv, ss, sm, se) = (
            S[k] for k in ["sd", "sg", "so", "so1", "sv", "ss", "sm", "se"])
        soP = [so, so1]
        if fctx is None:
            p = Prog(nc)
            p.autosem = {"vector": sv, "scalar": ss}
        else:
            p = fctx.p
        p.dma("sync", stt[0:64, :], st_in, sd)
        p.dma("sync", spc[0:64, :], sp_h, sd)
        p.dma("sync", szc[0:64, :], sz_h, sd)
        p.dma("sync", tdr[0:1, :], tdr_h, sd)
        p.dma("sync", posr[0:1, :], pos_h, sd)
        p.dma("gpsimd", wob[0:64, :], wop_h, sg)
        p.wait("tensor", sd)
        p.wait("tensor", sg)
        for tt in range(4):
            p.op("tensor", lambda E, tt=tt: E.matmul(
                psw[0:64, tt * 512:(tt + 1) * 512], tdr[0:1, :],
                posr[0:1, tt * 512:(tt + 1) * 512], start=True, stop=True),
                sm if tt == 3 else None)
        p.wait("scalar", sm)
        p.op("scalar", lambda E: E.activation(wp[0:64, :], psw[0:64, :], AF.Exp), ss)
        p.wait("vector", ss)
        p.op("vector", lambda E: E.tensor_tensor(spc[0:64, :], spc[0:64, :], szc[0:64, :], ALU.mult), sv)
        p.wait("vector", sv)
        p.op("vector", lambda E: E.tensor_scalar(wp[0:64, :], wp[0:64, :], spc[0:64, 0:1], None, ALU.mult), sv)
        p.op("vector", lambda E: E.tensor_tensor(stt[0:64, :], stt[0:64, :], wp[0:64, :], ALU.add), sv)
        p.wait("vector", sv)
        p.op("vector", lambda E: E.tensor_copy(stb[0:64, :], stt[0:64, :]), sv)
        p.wait("tensor", sv)
        for dc in range(16):
            for tt in range(4):
                p.op("tensor", lambda E, dc=dc, tt=tt: E.matmul(
                    pso[:, tt * 512:(tt + 1) * 512],
                    wob[0:64, dc * 128:(dc + 1) * 128],
                    stb[0:64, tt * 512:(tt + 1) * 512],
                    start=True, stop=True),
                    sm if tt == 3 else None)
            p.wait("vector", sm)
            db = dc % 2
            if dc >= 2:
                p.wait("vector", soP[db])
            p.op("vector", lambda E, db=db: E.tensor_copy(ob[:, db * T:(db + 1) * T], pso[:, :]), sv)
            p.wait("tensor", sv)
            p.wait("sync", sv)
            p.dma("sync", at_h[dc * 128:(dc + 1) * 128, :], ob[:, db * T:(db + 1) * T], soP[db])
        for s_ in (so, so1, sd, sg):
            p.wait("sync", s_)
        p.wait("sync", sg)
        if fctx is None:
            block = es.enter_context(nc.Block())
            p.run(block)
        else:
            _barrier(p, S)
    return nc



def build_ffn1final(debug=False, fctx=None, pre=""):
    nc = fctx.nc if fctx else _mk_nc()
    xn_h = _io(fctx, nc, pre + "xn2", [D, T], BF16, "ExternalInput")
    w1_h = _io(fctx, nc, pre + "w1T", [D, H1], F32, "ExternalInput")
    w2_h = _io(fctx, nc, pre + "w2T", [D, H1], F32, "ExternalInput")
    w3_h = _io(fctx, nc, pre + "w3T", [H1, D], F32, "ExternalInput")
    x1_h = _io(fctx, nc, pre + "x1", [D, T], F32, "ExternalInput")
    at_h = _io(fctx, nc, pre + "attn1", [D, T], F32, "ExternalInput")
    consts = _io(fctx, nc, pre + "consts", [128, 8], F32, "ExternalInput")
    onesc_h = _io(fctx, nc, pre + "onesc", [128, 1], F32, "ExternalInput")
    onesr_h = _io(fctx, nc, pre + "onesr", [1, 128], F32, "ExternalInput")
    y_h = _io(fctx, nc, pre + "y", [D, T], F32, "ExternalOutput")
    kind = "ExternalOutput" if debug else "Internal"
    hid_h = nc.dram_tensor("hidden", [H1, T], BF16, kind=kind)
    w3q_h = nc.dram_tensor("w3q", [H1, D], BF16, kind=kind)
    gdump_h = (nc.dram_tensor("gdump", [128, 4], F32, kind="ExternalOutput").ap()
               if debug else None)
    accdump_h = (nc.dram_tensor("accdump", [128, 4], F32, kind="ExternalOutput").ap()
                 if debug else None)
    sgmdump_h = (nc.dram_tensor("sgmdump", [128, 512], F32, kind="ExternalOutput").ap()
                 if debug else None)
    assert fctx is None or not debug
    es = ExitStack()
    with es:
        xn = es.enter_context(nc.sbuf_tensor(pre + "s_xn", [128, 16 * T], BF16))
        ct = es.enter_context(nc.sbuf_tensor(pre + "ct", [128, 8], F32))
        onesc = es.enter_context(nc.sbuf_tensor(pre + "s_onesc", [128, 1], F32))
        onesr = es.enter_context(nc.sbuf_tensor(pre + "s_onesr", [1, 128], F32))
        arena = es.enter_context(nc.sbuf_tensor(pre + "arena", [128, 16384], I32))
        acc = es.enter_context(nc.sbuf_tensor(pre + "acc", [128, 4], F32))
        ginv = es.enter_context(nc.sbuf_tensor(pre + "ginv", [128, 4], F32))
        grow = es.enter_context(nc.sbuf_tensor(pre + "grow", [1, 4], F32))
        w1q = es.enter_context(nc.sbuf_tensor(pre + "w1q", [128, 16 * 512], BF16))
        w2q = es.enter_context(nc.sbuf_tensor(pre + "w2q", [128, 16 * 512], BF16))
        sgm = es.enter_context(nc.sbuf_tensor(pre + "sgm", [128, 512], F32))
        hb = es.enter_context(nc.sbuf_tensor(pre + "hb", [128, 2 * T], BF16))
        w3b = es.enter_context(nc.sbuf_tensor(pre + "w3b", [128, 44 * 128], BF16))
        x1b = es.enter_context(nc.sbuf_tensor(pre + "x1b", [128, 512], F32))
        atb = es.enter_context(nc.sbuf_tensor(pre + "atb", [128, 512], F32))
        yb = es.enter_context(nc.sbuf_tensor(pre + "yb", [128, 2 * 512], F32))
        t1 = es.enter_context(nc.sbuf_tensor(pre + "t1", [128, 512], F32))
        ps1 = es.enter_context(nc.psum_tensor(pre + "ps1", [128, T], F32))
        ps2 = es.enter_context(nc.psum_tensor(pre + "ps2", [128, T], F32))
        S = fctx.S if fctx else _sems(es, nc, ALL_SEMS)
        (sd, sg, so, so1, sdp, sdh, sv, ss, sm, se) = (
            S[k] for k in ["sd", "sg", "so", "so1", "sdp", "sdh", "sv", "ss", "sm", "se"])
        soP = [so, so1]
        if fctx is None:
            p = Prog(nc)
            p.autosem = {"vector": sv, "scalar": ss}
        else:
            p = fctx.p
        wf = arena.ap().bitcast(F32)[:, 0:8192]
        qi = arena.ap()[:, 8192:16384]
        hidt = arena.ap().bitcast(BF16)[:, 0:22528]

        p.dma("sync", ct[:, :], consts, sd)
        p.dma("sync", onesc[:, :], onesc_h, sd)
        p.dma("sync", onesr[:, :], onesr_h, sd)
        for c in range(16):
            p.dma("sync", xn[:, c * T:(c + 1) * T], xn_h[c * 128:(c + 1) * 128, :], sd)

        for wi, (wh, nch, wid) in enumerate(((w1_h, 16, H1), (w2_h, 16, H1), (w3_h, 44, D))):
            for c in range(nch):
                p.wait("sync", sv, p.val(sv))
                p.dma("sync", wf[:, 0:wid], wh[c * 128:(c + 1) * 128, :], sd)
                p.wait("vector", sd)
                p.op("vector", lambda E, wid=wid, c=c: E.reduce_sum(
                    sgm[:, c:c + 1], wf[:, 0:wid], mybir.AxisListType.X,
                    apply_absolute_value=True), sv)
            p.wait("vector", sv)
            p.op("vector", lambda E, wi=wi, nch=nch: E.reduce_sum(
                acc[:, wi:wi + 1], sgm[:, 0:nch], mybir.AxisListType.X), sv)
            p.wait("vector", sv)
        p.wait("tensor", sv)
        p.op("tensor", lambda E: E.matmul(ps1[0:1, 0:3], onesc[:, :], acc[:, 0:3],
                                          start=True, stop=True), sm)
        p.wait("vector", sm)
        p.op("vector", lambda E: E.tensor_scalar(
            grow[0:1, 0:2], ps1[0:1, 0:2], 1.0 / (D * H1), 1e-8, ALU.mult, ALU.add), sv)
        p.op("vector", lambda E: E.tensor_scalar(
            grow[0:1, 2:3], ps1[0:1, 2:3], 1.0 / (D * H1), 1e-8, ALU.mult, ALU.add), sv)
        p.wait("vector", sv)
        p.op("vector", lambda E: E.reciprocal(grow[0:1, 0:3], grow[0:1, 0:3]), sv)
        p.wait("vector", sv)
        p.wait("tensor", sv)
        p.op("tensor", lambda E: E.matmul(ps2[:, 0:3], onesr[0:1, :], grow[0:1, 0:3],
                                          start=True, stop=True), sm)
        p.wait("vector", sm)
        p.op("vector", lambda E: E.tensor_copy(ginv[:, 0:3], ps2[:, 0:3]), sv)
        p.marks["f1_psum_free"] = p.val(sv)
        if debug:
            p.wait("sync", sv)
            p.dma("sync", gdump_h, ginv[:, :], so)
            p.dma("sync", accdump_h, acc[:, :], so)
            p.dma("sync", sgmdump_h, sgm[:, :], so)

        SO_G = p.val(S["so"])
        def quant(dst, src, gcol, n):
            p.op("vector", lambda E: E.tensor_scalar(
                qi[:, 0:n], src, ginv[:, gcol:gcol + 1], None, ALU.mult), sv)
            p.op("vector", lambda E: E.tensor_scalar(
                dst, qi[:, 0:n], -1.0, 1.0, ALU.max, ALU.min), sv)

        for c in range(44):
            p.wait("sync", sv, p.val(sv))
            p.dma("sync", wf[:, 0:D], w3_h[c * 128:(c + 1) * 128, :], sd)
            p.wait("vector", sd)
            if c >= 1:
                p.wait("vector", so)
            quant(hb[:, 0:D], wf[:, 0:D], 2, D)
            p.wait("sync", sv)
            p.dma("sync", w3q_h[c * 128:(c + 1) * 128, :], hb[:, 0:D], so)

        p.wait("vector", so)
        for hs in range(11):
            h0 = hs * 512
            for wi, (wh, wq) in enumerate(((w1_h, w1q), (w2_h, w2q))):
                p.wait("sync", sv, p.val(sv))
                p.dma("sync", wf[:, 0:16 * 512],
                      wh.rearrange("(dc k) h -> k dc h", k=128)[:, :, h0:h0 + 512], sd)
                p.wait("vector", sd)
                p.wait("vector", sg)
                p.wait("vector", sm)
                quant(wq[:, :], wf[:, 0:16 * 512], wi, 16 * 512)
            p.wait("tensor", sv)
            for hsub in range(4):
                hc0 = hsub * 128
                p.wait("tensor", sv, p.marks.get("f1_psum_free", 0))
                for ps_, wq in ((ps1, w1q), (ps2, w2q)):
                    for tt in range(4):
                        for dc in range(16):
                            p.op("tensor", lambda E, ps_=ps_, wq=wq, tt=tt, dc=dc, hc0=hc0: E.matmul(
                                ps_[:, tt * 512:(tt + 1) * 512],
                                wq[:, dc * 512 + hc0:dc * 512 + hc0 + 128],
                                xn[:, dc * T + tt * 512:dc * T + (tt + 1) * 512],
                                start=(dc == 0), stop=(dc == 15)),
                                sm if (dc == 15) else None)
                i = hs * 4 + hsub
                ib = i % 2
                p.wait("scalar", sm)
                if i >= 2:
                    p.wait("vector", soP[ib])
                for tt in range(4):
                    p.wait("scalar", sv, p.val(sv))
                    p.op("scalar", lambda E, tt=tt: E.activation(
                        sgm[:, :], ps1[:, tt * 512:(tt + 1) * 512], AF.Sigmoid), ss)
                    p.wait("vector", ss)
                    p.op("vector", lambda E, ib=ib, tt=tt: E.tensor_tensor(
                        hb[:, ib * T + tt * 512:ib * T + (tt + 1) * 512],
                        sgm[:, :], ps2[:, tt * 512:(tt + 1) * 512], ALU.mult), sv)
                p.marks["f1_psum_free"] = p.val(sv)
                p.wait("sync", sv)
                p.dma("sync", hid_h[h0 + hc0:h0 + hc0 + 128, :],
                      hb[:, ib * T:(ib + 1) * T], soP[ib])

        for s_ in (so, so1):
            p.wait("sync", s_)
            p.wait("gpsimd", s_)
        for tt in range(4):
            p.wait("sync", sm)
            p.dma("sync", hidt[:, :],
                  hid_h.rearrange("(hcc k) t -> k hcc t", k=128)[:, :, tt * 512:(tt + 1) * 512], sdh)
            for dc in range(16):
                i = tt * 16 + dc
                p.wait("gpsimd", sm)
                p.dma("gpsimd", w3b[:, :],
                      w3q_h.rearrange("(hcc k) d -> k hcc d", k=128)[:, :, dc * 128:(dc + 1) * 128], sg)
                p.wait("sync", sv, p.val(sv))
                p.dma("sync", x1b[:, :], x1_h[dc * 128:(dc + 1) * 128, tt * 512:(tt + 1) * 512], sdp)
                p.dma("sync", atb[:, :], at_h[dc * 128:(dc + 1) * 128, tt * 512:(tt + 1) * 512], sdp)
                p.wait("tensor", sdh)
                p.wait("tensor", sg)
                p.wait("tensor", ss, p.marks.get("f1d_psum_free", 0))
                for hcc in range(44):
                    p.op("tensor", lambda E, hcc=hcc: E.matmul(
                        ps1[:, 0:512], w3b[:, hcc * 128:(hcc + 1) * 128],
                        hidt[:, hcc * 512:(hcc + 1) * 512],
                        start=(hcc == 0), stop=(hcc == 43)),
                        sm if hcc == 43 else None)
                p.wait("scalar", sm)
                p.wait("scalar", sv, p.val(sv))
                p.op("scalar", lambda E: E.activation(t1[:, :], ps1[:, 0:512], AF.Copy), ss)
                p.marks["f1d_psum_free"] = p.val(ss)
                p.wait("vector", ss)
                p.wait("vector", sdp)
                b = i % 2
                if i >= 2:
                    p.wait("vector", soP[b])
                p.op("vector", lambda E, b=b: E.tensor_tensor(
                    yb[:, b * 512:(b + 1) * 512], t1[:, :], x1b[:, :], ALU.add), sv)
                p.op("vector", lambda E, b=b: E.tensor_tensor(
                    yb[:, b * 512:(b + 1) * 512], yb[:, b * 512:(b + 1) * 512],
                    atb[:, :], ALU.add), sv)
                p.wait("sync", sv)
                p.dma("sync", y_h[dc * 128:(dc + 1) * 128, tt * 512:(tt + 1) * 512],
                      yb[:, b * 512:(b + 1) * 512], soP[b])
        for s_ in (so, so1, sd, sg, sdp, sdh):
            p.wait("sync", s_)
        if fctx is None:
            block = es.enter_context(nc.Block())
            p.run(block)
        else:
            _barrier(p, S)
    return nc





def emit_exchange(p, S, nc, tiles, idx, src_dram, dst_dram, sh_data, sh_flag,
                  rows, cols, magic):
    sx = S["sx"]
    spl = S[f"spl{idx}"]
    ht, mgt, flt = tiles["ht"], tiles["mgt"], tiles["flt"]
    elems = rows * cols
    t1 = p._bump(sx, 16)
    t2 = p._bump(sx, 16)
    t3 = p._bump(sx, 16)
    pat = [[cols, rows], [1, cols]]

    def go(E, src=src_dram, dst=dst_dram, elems=elems, idx=idx, magic=magic,
           t1=t1, t2=t2, t3=t3, pat=pat):
        with E.register(f"h{idx}") as h, E.register(f"off{idx}") as off, \
             E.register(f"poff{idx}") as poff, E.register(f"ptgt{idx}") as ptgt, \
             E.register(f"fv{idx}") as fv, E.register(f"cnd{idx}") as cnd, \
             E.register(f"foff{idx}") as foff:
            E.reg_load(h, ht[:1, :1])
            E.reg_mul(off, h, elems)
            E.dma_start(out=bass.AP(sh_data, off, pat), in_=src).then_inc(sx, 16)
            E.wait_ge(sx, t1)
            E.reg_mul(foff, h, 4)
            E.reg_add(foff, foff, idx)
            E.dma_start(out=bass.AP(sh_flag, foff, [[1, 1], [1, 1]]),
                        in_=mgt[0:1, idx:idx + 1]).then_inc(sx, 16)
            E.wait_ge(sx, t2)
            E.reg_sub(poff, 1, h)
            E.reg_mul(foff, poff, 4)
            E.reg_add(foff, foff, idx)
            E.reg_mov(fv, 0)
            E.reg_mov(ptgt, 16)
            E.reg_mov(cnd, 1)
            pflag = bass.AP(sh_flag, foff, [[1, 1], [1, 1]])
            with E.While(cnd):
                E.dma_start(out=flt[:1, :1], in_=pflag).then_inc(spl, 16)
                E.wait_ge(spl, ptgt)
                E.reg_load(fv, flt[:1, :1])
                E.reg_add(ptgt, ptgt, 16)
                E.reg_alu(cnd, fv, magic, mybir.AluOpType.not_equal)
            E.reg_mul(off, poff, elems)
            return E.dma_start(out=dst, in_=bass.AP(sh_data, off, pat))
    p.on("gpsimd", lambda E: go(E).then_inc(S["sx"], 16))


def build_fused():
    nc = _mk_nc()
    S_es = ExitStack()
    S = {n: S_es.enter_context(nc.semaphore(n)) for n in ALL_SEMS}
    p = Prog(nc)
    p.autosem = {"vector": S["sv"], "scalar": S["ss"]}
    fctx = FCtx(nc, p, S)
    io = fctx.io

    def ext(name, shape, dt):
        io_ap = nc.dram_tensor(name, shape, dt, kind="ExternalInput").ap()
        return io_ap

    def internal(name, shape, dt):
        return nc.dram_tensor(name, shape, dt, kind="Internal").ap()

    x0 = ext("x0", [D, T], F32)
    consts = ext("consts", [128, 8], F32)
    onesc = ext("onesc", [128, 1], F32)
    onesr = ext("onesr", [1, 128], F32)
    onescb = ext("onescb", [128, 1], BF16)
    winT = ext("winT", [D, 3 * D], F32)
    binc = ext("binc", [128, 48], F32)
    cosr = ext("cosr", [512, T], F32)
    sinr = ext("sinr", [512, T], F32)
    woutT = ext("woutT", [D, D], F32)
    boutc = ext("boutc", [128, 16], F32)
    gpk = ext("gpk", [1024, NF], I32)
    gsc = ext("gsc", [64, NF], F32)
    upk = ext("upk", [1024, NF], I32)
    usc = ext("usc", [64, NF], F32)
    dpk = ext("dpk", [NF // 2, D], I32)
    dsc = ext("dsc", [512, D], F32)
    wshT = ext("wshT", [D, D], F32)
    gatec = ext("gatec", [128, 16], F32)
    lngc = ext("lngc", [128, 16], F32)
    lnbc = ext("lnbc", [128, 16], F32)
    wkvT = ext("wkvT", [D, 128], F32)
    tdc = ext("tdc", [64, 1], F32)
    tfc = ext("tfc", [64, 1], F32)
    tdrow = ext("tdrow", [1, 64], F32)
    posr = ext("posr", [1, T], F32)
    w1T = ext("w1T", [D, H1], F32)
    w2T = ext("w2T", [D, H1], F32)
    w3T = ext("w3T", [H1, D], F32)
    woutpT = ext("woutpT", [64, D], F32)
    szero = ext("szero", [64, 1], F32)
    halfI = ext("halfI", [1, 1], I32)
    magics = ext("magics", [1, 3], I32)
    y = nc.dram_tensor("y", [D, T], F32, kind="ExternalOutput").ap()

    xn = internal("i_xn", [D, T], BF16)
    apart = internal("i_apart", [128, 16], F32)
    apart_p = internal("i_apart_p", [128, 16], F32)
    px1 = internal("i_px1", [D, T], F32)
    ddec = internal("i_ddec", [NF, D], BF16)
    inter = internal("i_inter", [NF, T], BF16)
    x1 = internal("i_x1", [D, T], F32)
    xn2 = internal("i_xn2", [D, T], BF16)
    z = internal("i_z", [D, T], BF16)
    zp = internal("i_zp", [D, T], BF16)
    states = internal("i_states", [64, T], F32)
    sfin = internal("i_sfin", [64, 1], F32)
    sfin_p = internal("i_sfin_p", [64, 1], F32)
    attn1 = internal("i_attn1", [D, T], F32)
    sh_ap = nc.dram_tensor("sh_ap", [2, 128 * 16], F32, addr_space="Shared")
    sh_z = nc.dram_tensor("sh_z", [2, D * T], BF16, addr_space="Shared")
    sh_sf = nc.dram_tensor("sh_sf", [2, 64], F32, addr_space="Shared")
    sh_flag = nc.dram_tensor("sh_flag", [2, 4], I32, addr_space="Shared")

    ex_es = ExitStack()
    ht = ex_es.enter_context(nc.sbuf_tensor("x_ht", [1, 1], I32))
    mgt = ex_es.enter_context(nc.sbuf_tensor("x_mgt", [1, 3], I32))
    flt = ex_es.enter_context(nc.sbuf_tensor("x_flt", [1, 1], I32))
    extiles = dict(ht=ht, mgt=mgt, flt=flt)
    p.dma("gpsimd", ht[:, :], halfI, S["sg"])
    p.dma("gpsimd", mgt[:, :], magics, S["sg"])
    p.wait("gpsimd", S["sg"])

    io.update({"ln0_xp0": x0, "ln0_consts": consts, "ln0_onesc": onesc,
               "ln0_onesr": onesr, "ln0_xn": xn})
    build_ln(1, fctx=fctx, pre="ln0_")
    io.update({"kv_xn": xn, "kv_winT": winT, "kv_binc": binc, "kv_cosr": cosr,
               "kv_sinr": sinr, "kv_consts": consts, "kv_onesc": onesc,
               "kv_onesr": onesr, "kv_apart": apart})
    build_kva(fctx=fctx, pre="kv_")
    emit_exchange(p, S, nc, extiles, 0, apart, apart_p, sh_ap, sh_flag,
                  128, 16, MAGIC0)
    _barrier(p, S)
    io.update({"qa_xn": xn, "qa_x0": x0, "qa_winT": winT, "qa_binc": binc,
               "qa_cosr": cosr, "qa_sinr": sinr, "qa_woutT": woutT,
               "qa_boutc": boutc, "qa_amine": apart, "qa_apartner": apart_p,
               "qa_px1": px1})
    build_qattn(fctx=fctx, pre="qa_")
    io.update({"f0a_xn": xn, "f0a_gpk": gpk, "f0a_gsc": gsc, "f0a_upk": upk,
               "f0a_usc": usc, "f0a_dpk": dpk, "f0a_dsc": dsc,
               "f0a_consts": consts, "f0a_ddec": ddec, "f0a_inter": inter})
    build_ffn0a(fctx=fctx, pre="f0a_")
    io.update({"f0b_inter": inter, "f0b_ddec": ddec, "f0b_px1": px1,
               "f0b_x1": x1})
    build_ffn0b(fctx=fctx, pre="f0b_")
    io.update({"ln1_xp0": x1, "ln1_consts": consts, "ln1_onesc": onesc,
               "ln1_onesr": onesr, "ln1_xn": xn2})
    build_ln(1, fctx=fctx, pre="ln1_")
    io.update({"z_xn2": xn2, "z_wshT": wshT, "z_z": z})
    build_zmm(fctx=fctx, pre="z_")
    emit_exchange(p, S, nc, extiles, 1, z, zp, sh_z, sh_flag, D, T, MAGIC1)
    _barrier(p, S)
    io.update({"l1_xn2": xn2, "l1_zp": zp, "l1_gatec": gatec, "l1_lngc": lngc,
               "l1_lnbc": lnbc, "l1_wkvT": wkvT, "l1_tdc": tdc, "l1_tfc": tfc,
               "l1_consts": consts, "l1_onesc": onesc, "l1_onescb": onescb,
               "l1_onesr": onesr, "l1_states": states, "l1_sfin": sfin})
    build_l1attn(fctx=fctx, pre="l1_")
    emit_exchange(p, S, nc, extiles, 2, sfin, sfin_p, sh_sf, sh_flag, 64, 1, MAGIC2)
    _barrier(p, S)
    io.update({"pj_states": states, "pj_sp": sfin_p, "pj_szero": szero,
               "pj_tdrow": tdrow, "pj_posr": posr, "pj_woutpT": woutpT,
               "pj_attn1": attn1})
    build_l1proj(fctx=fctx, pre="pj_")
    io.update({"f1_xn2": xn2, "f1_w1T": w1T, "f1_w2T": w2T, "f1_w3T": w3T,
               "f1_x1": x1, "f1_attn1": attn1, "f1_consts": consts,
               "f1_onesc": onesc, "f1_onesr": onesr, "f1_y": y})
    build_ffn1final(fctx=fctx, pre="f1_")

    block = S_es.enter_context(nc.Block())
    p.run(block)
    S_es.close()
    ex_es.close()
    return nc



_P = np.concatenate([np.arange(0, D, 2), np.arange(1, D, 2)])
_PINV = np.argsort(_P)
_R = np.concatenate([np.arange(0, 1024, 2), np.arange(1, 1024, 2), np.arange(1024, D)])
_NP = np.concatenate([np.arange(0, NF, 2), np.arange(1, NF, 2)])

_NC_CACHE = {}


def _nc(key, builder, *a, **kw):
    if key not in _NC_CACHE:
        _NC_CACHE[key] = builder(*a, **kw)
    return _NC_CACHE[key]


def _cc(x):
    return np.ascontiguousarray(x)


def kernel(L0_attn_X, L0_attn_W_in, L0_attn_b_in, L0_attn_W_out, L0_attn_b_out,
           L0_ffn_gate_W_packed, L0_ffn_gate_scales, L0_ffn_up_W_packed,
           L0_ffn_up_scales, L0_ffn_down_W_packed, L0_ffn_down_scales,
           L1_attn_W_key, L1_attn_W_value, L1_attn_W_output, L1_attn_W_shift,
           L1_attn_gate, L1_attn_time_decay, L1_attn_time_first,
           L1_attn_ln_gamma, L1_attn_ln_beta, L1_ffn_w1, L1_ffn_w2, L1_ffn_w3):
    f32 = np.float32
    X = np.asarray(L0_attn_X, f32)
    cores = list(range(8))

    rows_qkv = np.concatenate([_R, D + _R, 2 * D + _R])
    winT = _cc(np.asarray(L0_attn_W_in, f32)[np.ix_(rows_qkv, _P)].T)
    binc = _cc(np.asarray(L0_attn_b_in, f32)[rows_qkv].reshape(48, 128).T)
    woutT = _cc(np.asarray(L0_attn_W_out, f32)[np.ix_(_P, _R)].T)
    boutc = _cc(np.asarray(L0_attn_b_out, f32)[_P].reshape(16, 128).T)
    gpk = _cc(np.asarray(L0_ffn_gate_W_packed)[_NP].T.astype(np.int32))
    gsc = _cc(np.asarray(L0_ffn_gate_scales, f32)[_NP].T)
    upk = _cc(np.asarray(L0_ffn_up_W_packed)[_NP].T.astype(np.int32))
    usc = _cc(np.asarray(L0_ffn_up_scales, f32)[_NP].T)
    dpk = _cc(np.asarray(L0_ffn_down_W_packed)[_P].T.astype(np.int32))
    dsc = _cc(np.asarray(L0_ffn_down_scales, f32)[_P].T)
    wshT = _cc(np.asarray(L1_attn_W_shift, f32)[np.ix_(_P, _P)].T)
    gatec = _cc(np.asarray(L1_attn_gate, f32)[_P].reshape(16, 128).T)
    lngc = _cc(np.asarray(L1_attn_ln_gamma, f32)[_P].reshape(16, 128).T)
    lnbc = _cc(np.asarray(L1_attn_ln_beta, f32)[_P].reshape(16, 128).T)
    wkvT = _cc(np.concatenate([np.asarray(L1_attn_W_key, f32)[:, _P].T,
                               np.asarray(L1_attn_W_value, f32)[:, _P].T], axis=1))
    woutpT = _cc(np.asarray(L1_attn_W_output, f32)[_P].T)
    tdc = _cc(np.asarray(L1_attn_time_decay, f32).reshape(64, 1))
    tfc = _cc(np.asarray(L1_attn_time_first, f32).reshape(64, 1))
    tdrow = _cc(np.asarray(L1_attn_time_decay, f32).reshape(1, 64))
    w1T = _cc(np.asarray(L1_ffn_w1, f32)[:, _P].T)
    w2T = _cc(np.asarray(L1_ffn_w2, f32)[:, _P].T)
    w3T = _cc(np.asarray(L1_ffn_w3, f32)[_P].T)

    consts = np.zeros((128, 8), f32)
    consts[:, C_ONE] = 1.0
    consts[:, C_EPS] = 1e-5
    consts[:, C_75] = 7.5
    consts[:, C_KEPS] = 1e-24
    consts[:, C_EPS8] = 1e-8
    consts[:, C_HALF] = 0.5
    onesc = np.ones((128, 1), f32)
    onesr = np.ones((1, 128), f32)
    onescb = np.ones((128, 1), ml_dtypes.bfloat16)
    posr = np.arange(1, T + 1, dtype=f32).reshape(1, T)
    invf = (1.0 / 10000.0 ** (np.arange(0, 1024, 2, dtype=f32) / 1024.0)).astype(f32)
    cossin = []
    for half in range(2):
        pos = (half * T + np.arange(T, dtype=f32))
        fr = pos[None, :] * invf[:, None]
        cossin.append((np.cos(fr).astype(f32), np.sin(fr).astype(f32)))

    x0 = [_cc(X[c // 2, (c % 2) * T:(c % 2) * T + T, :].T[_P]) for c in cores]
    szero = [np.full((64, 1), float(c % 2), f32) for c in cores]

    nc = _nc("fused", build_fused)
    magics = np.array([[MAGIC0, MAGIC1, MAGIC2]], np.int32)
    in_maps = []
    for c in cores:
        in_maps.append(dict(
            x0=x0[c], consts=consts, onesc=onesc, onesr=onesr, onescb=onescb,
            winT=winT, binc=binc, cosr=cossin[c % 2][0], sinr=cossin[c % 2][1],
            woutT=woutT, boutc=boutc, gpk=gpk, gsc=gsc, upk=upk, usc=usc,
            dpk=dpk, dsc=dsc, wshT=wshT, gatec=gatec, lngc=lngc, lnbc=lnbc,
            wkvT=wkvT, tdc=tdc, tfc=tfc, tdrow=tdrow, posr=posr,
            w1T=w1T, w2T=w2T, w3T=w3T, woutpT=woutpT, szero=szero[c],
            halfI=np.array([[c % 2]], np.int32), magics=magics))
    res = run_bass_kernel_spmd(nc, in_maps, core_ids=cores)
    y = [res.results[c]["y"] for c in cores]
    out = np.empty((4, 4096, D), f32)
    for c in cores:
        out[c // 2, (c % 2) * T:(c % 2) * T + T, :] = y[c][_PINV].T
    return out


# version 05
# speedup vs baseline: 1.3792x; 1.3792x over previous
import numpy as np
import ml_dtypes
from contextlib import ExitStack

import concourse.bass as bass
import concourse.mybir as mybir
from concourse.bass_utils import run_bass_kernel_spmd

F32 = mybir.dt.float32
BF16 = mybir.dt.bfloat16
I32 = mybir.dt.int32
AF = mybir.ActivationFunctionType
ALU = mybir.AluOpType

D = 2048
T = 2048
NF = 16384
H1 = 5632
DS = 64

ENGINES = ("sync", "gpsimd", "vector", "scalar", "tensor")

MAGIC0, MAGIC1, MAGIC2 = 0x5A00111, 0x5A00222, 0x5A00333

C_ZERO, C_ONE, C_EPS, C_75, C_KEPS, C_EPS8, C_HALF, C_X = range(8)


class Prog:

    def __init__(self, nc):
        self.nc = nc
        self.ops = {e: [] for e in ENGINES}
        self.semctr = {}
        self.marks = {}
        self.autosem = {}

    def on(self, eng, fn):
        self.ops[eng].append(fn)

    def _bump(self, sem, n):
        self.semctr[sem] = self.semctr.get(sem, 0) + n
        return self.semctr[sem]

    def val(self, sem):
        return self.semctr.get(sem, 0)

    def op(self, eng, fn, sem=None, n=1):
        auto = self.autosem.get(eng) if hasattr(self, "autosem") else None
        if auto is not None:
            v = self.val(auto)
            if v > 0:
                self.on(eng, lambda E, s=auto, v=v: E.wait_ge(s, v))
            if sem is None:
                sem, n = auto, 1
            else:
                assert sem is auto, (eng, "non-default sem on auto-synced engine")
        if sem is None:
            self.on(eng, fn)
            return None
        tgt = self._bump(sem, n)
        self.on(eng, lambda E, f=fn, s=sem, n=n: f(E).then_inc(s, n))
        return tgt

    def wait(self, eng, sem, v=None):
        if v is None:
            v = self.val(sem)
        if v > 0:
            self.on(eng, lambda E, s=sem, v=v: E.wait_ge(s, v))

    def dma(self, eng, out, in_, sem, **kw):
        return self.op(eng, lambda E, o=out, i=in_, kw=kw:
                       E.dma_start(out=o, in_=i, **kw), sem, 16)

    def run(self, block):
        for e in ENGINES:
            stream = self.ops[e]
            if stream:
                getattr(block, e)(lambda E, s=stream: [f(E) for f in s])




ALL_SEMS = ["sd", "sg", "sg1", "so", "so1", "sv", "ss", "sm", "se",
            "sk0", "sk1", "sc0", "sc1", "st0", "st1", "si0", "si1",
            "sdp", "sdh", "sx", "spl0", "spl1", "spl2"]
POLL_SEMS = {"spl0", "spl1", "spl2"}


class FCtx:

    def __init__(self, nc, p, S):
        self.nc, self.p, self.S = nc, p, S
        self.io = {}


def _phase(fctx, names):
    return fctx.nc, fctx.S, fctx.p


def _io(fctx, nc, name, shape, dtype, kind):
    if fctx is None:
        return nc.dram_tensor(name, shape, dtype, kind=kind).ap()
    return fctx.io[name]


def _barrier(p, S):
    for e in ENGINES:
        for k, s in S.items():
            if k in POLL_SEMS:
                continue
            p.wait(e, s)


def _mk_nc():
    return bass.Bass("TRN2", target_bir_lowering=False, debug=False)


def _sems(es, nc, names):
    return {n: es.enter_context(nc.semaphore(n)) for n in names}



def build_ln(n_parts, out_x=False, fctx=None, pre=""):
    nc = fctx.nc if fctx else _mk_nc()
    xs_h = [_io(fctx, nc, pre + f"xp{i}", [D, T], F32, "ExternalInput")
            for i in range(n_parts)]
    consts = _io(fctx, nc, pre + "consts", [128, 8], F32, "ExternalInput")
    onesc_h = _io(fctx, nc, pre + "onesc", [128, 1], F32, "ExternalInput")
    onesr_h = _io(fctx, nc, pre + "onesr", [1, 128], F32, "ExternalInput")
    xn_h = _io(fctx, nc, pre + "xn", [D, T], BF16, "ExternalOutput")
    xsum_h = (_io(fctx, nc, pre + "xsum", [D, T], F32, "ExternalOutput")
              if out_x else None)
    es = ExitStack()
    with es:
        xs = es.enter_context(nc.sbuf_tensor(pre + "xs", [128, 16 * T], F32))
        xtmp = es.enter_context(nc.sbuf_tensor(pre + "xtmp", [128, T], F32))
        sq = es.enter_context(nc.sbuf_tensor(pre + "sq", [128, T], F32))
        ct = es.enter_context(nc.sbuf_tensor(pre + "ct", [128, 8], F32))
        onesc = es.enter_context(nc.sbuf_tensor(pre + "s_onesc", [128, 1], F32))
        onesr = es.enter_context(nc.sbuf_tensor(pre + "s_onesr", [1, 128], F32))
        srow = es.enter_context(nc.sbuf_tensor(pre + "srow", [1, 4 * T], F32))
        bc = es.enter_context(nc.sbuf_tensor(pre + "bc", [128, 2 * T], F32))
        xnb = es.enter_context(nc.sbuf_tensor(pre + "xnb", [128, 2 * T], BF16))
        ps = es.enter_context(nc.psum_tensor(pre + "ps", [1, T], F32))
        psb = es.enter_context(nc.psum_tensor(pre + "psb", [128, 512], F32))
        S = fctx.S if fctx else _sems(es, nc, ALL_SEMS)
        sd, sg, so, so1, sv, ss, sm = (S[k] for k in
                                       ["sd", "sg", "so", "so1", "sv", "ss", "sm"])
        soP = [so, so1]
        if fctx is None:
            p = Prog(nc)
            p.autosem = {"vector": sv, "scalar": ss}
        else:
            p = fctx.p

        p.dma("sync", ct[:, :], consts, sd)
        p.dma("sync", onesc[:, :], onesc_h, sd)
        p.dma("sync", onesr[:, :], onesr_h, sd)
        for c in range(16):
            p.dma("sync", xs[:, c * T:(c + 1) * T], xs_h[0][c * 128:(c + 1) * 128, :], sd)
        for ip in range(1, n_parts):
            for c in range(16):
                p.dma("sync", xtmp[:, :], xs_h[ip][c * 128:(c + 1) * 128, :], sd)
                p.wait("vector", sd)
                p.wait("vector", sg)
                p.op("vector", lambda E, c=c: E.tensor_tensor(
                    xs[:, c * T:(c + 1) * T], xs[:, c * T:(c + 1) * T],
                    xtmp[:, :], ALU.add), sv)
                p.wait("sync", sv)
        if out_x:
            p.wait("sync", sv)
            for c in range(16):
                p.dma("sync", xsum_h[c * 128:(c + 1) * 128, :],
                      xs[:, c * T:(c + 1) * T], so)
        p.wait("tensor", sd)
        p.wait("tensor", sg)
        p.wait("tensor", sv)
        for c in range(16):
            for tt in range(4):
                p.op("tensor", lambda E, c=c, tt=tt: E.matmul(
                    ps[0:1, tt * 512:(tt + 1) * 512], onesc[:, :],
                    xs[:, c * T + tt * 512:c * T + (tt + 1) * 512],
                    start=(c == 0), stop=(c == 15)),
                    sm if (c == 15 and tt == 3) else None)
        p.wait("vector", sm)
        MU, RS, NB, V1 = 0, 1, 2, 3
        p.op("vector", lambda E: E.tensor_scalar(
            srow[0:1, MU * T:(MU + 1) * T], ps[0:1, :], 1.0 / D, None, ALU.mult), sv)
        p.wait("scalar", sv)
        for c in range(16):
            p.wait("scalar", sm, p.val(sm))
            p.op("scalar", lambda E, c=c: E.activation(
                sq[:, :], xs[:, c * T:(c + 1) * T], AF.Square), ss)
            p.wait("tensor", ss)
            for tt in range(4):
                p.op("tensor", lambda E, c=c, tt=tt: E.matmul(
                    ps[0:1, tt * 512:(tt + 1) * 512], onesc[:, :],
                    sq[:, tt * 512:(tt + 1) * 512],
                    start=(c == 0), stop=(c == 15)),
                    sm if (tt == 3) else None)
        p.wait("vector", sm)
        p.op("vector", lambda E: E.tensor_scalar(
            srow[0:1, V1 * T:(V1 + 1) * T], ps[0:1, :], 1.0 / D, None, ALU.mult), sv)
        p.wait("vector", sv)
        p.op("vector", lambda E: E.tensor_tensor(
            srow[0:1, RS * T:(RS + 1) * T], srow[0:1, MU * T:(MU + 1) * T],
            srow[0:1, MU * T:(MU + 1) * T], ALU.mult), sv)
        p.wait("vector", sv)
        p.op("vector", lambda E: E.tensor_tensor(
            srow[0:1, V1 * T:(V1 + 1) * T], srow[0:1, V1 * T:(V1 + 1) * T],
            srow[0:1, RS * T:(RS + 1) * T], ALU.subtract), sv)
        p.wait("scalar", sv)
        p.op("scalar", lambda E: E.activation(
            srow[0:1, RS * T:(RS + 1) * T], srow[0:1, V1 * T:(V1 + 1) * T],
            AF.Ln, bias=ct[0:1, C_EPS:C_EPS + 1]), ss)
        p.wait("scalar", ss)
        p.op("scalar", lambda E: E.activation(
            srow[0:1, RS * T:(RS + 1) * T], srow[0:1, RS * T:(RS + 1) * T],
            AF.Exp, scale=-0.5), ss)
        p.wait("vector", ss)
        p.op("vector", lambda E: E.tensor_tensor(
            srow[0:1, NB * T:(NB + 1) * T], srow[0:1, MU * T:(MU + 1) * T],
            srow[0:1, RS * T:(RS + 1) * T], ALU.mult), sv)
        p.wait("vector", sv)
        p.op("vector", lambda E: E.tensor_scalar(
            srow[0:1, NB * T:(NB + 1) * T], srow[0:1, NB * T:(NB + 1) * T],
            -1.0, None, ALU.mult), sv)
        p.wait("vector", sv)
        p.wait("tensor", sv)
        for half, col in ((0, RS), (1, NB)):
            for tt in range(4):
                p.op("tensor", lambda E, col=col, tt=tt: E.matmul(
                    psb[:, :], onesr[:, :],
                    srow[0:1, col * T + tt * 512:col * T + (tt + 1) * 512],
                    start=True, stop=True), sm)
                p.wait("vector", sm)
                p.op("vector", lambda E, half=half, tt=tt: E.tensor_copy(
                    bc[:, half * T + tt * 512:half * T + (tt + 1) * 512],
                    psb[:, :]), sv)
                p.wait("tensor", sv)
        for c in range(16):
            buf = c % 2
            if c >= 2:
                p.wait("vector", soP[buf])
            p.op("vector", lambda E, c=c: E.tensor_tensor(
                sq[:, :], xs[:, c * T:(c + 1) * T], bc[:, 0:T], ALU.mult), sv)
            p.op("vector", lambda E, c=c, buf=buf: E.tensor_tensor(
                xnb[:, buf * T:(buf + 1) * T], sq[:, :], bc[:, T:2 * T], ALU.add), sv)
            p.wait("sync", sv)
            p.dma("sync", xn_h[c * 128:(c + 1) * 128, :],
                  xnb[:, buf * T:(buf + 1) * T], soP[buf])
        p.wait("sync", so)
        p.wait("sync", so1)
        p.wait("sync", sd)
        p.wait("sync", sg)
        if fctx is None:
            block = es.enter_context(nc.Block())
            p.run(block)
        else:
            _barrier(p, S)
    return nc



def _emit_qkv_chunk(p, tiles, ecs, blk_off, rope_pair, fuse, S):
    nc = p.nc
    sd, sg, sv, ss, sm, se = S["sd"], S["sg"], S["sv"], S["ss"], S["sm"], S["se"]
    xn, wb = tiles["xn"], tiles["wb"]
    winT = tiles["winT_h"]
    pss = [tiles["ps0"], tiles["ps1"]]
    t12 = [tiles["t1"], tiles["t2"]]
    binc = tiles["binc"]
    p.wait("gpsimd", sm)
    for j, ec in enumerate(ecs):
        col = blk_off + ec * 128
        p.dma("gpsimd", wb[:, j * 2048:(j + 1) * 2048],
              winT.rearrange("(dc k) e -> k dc e", k=128)[:, :, col:col + 128], sg)
    p.wait("tensor", sd)
    p.wait("tensor", sg)
    p.wait("tensor", sv, p.marks.get("qkv_psum_free", 0))
    for j, ec in enumerate(ecs):
        for tt in range(4):
            for dc in range(16):
                p.op("tensor", lambda E, j=j, dc=dc, tt=tt: E.matmul(
                    pss[j][:, tt * 512:(tt + 1) * 512],
                    wb[:, j * 2048 + dc * 128:j * 2048 + (dc + 1) * 128],
                    xn[:, dc * T + tt * 512:dc * T + (tt + 1) * 512],
                    start=(dc == 0), stop=(dc == 15)),
                    sm if (dc == 15 and tt == 3) else None)
    p.wait("scalar", sm)
    p.wait("scalar", sv)
    for j, ec in enumerate(ecs):
        bcol = (blk_off + ec * 128) // 128
        p.op("scalar", lambda E, j=j, bcol=bcol: E.activation(
            t12[j][:, :], pss[j][:, :], AF.Identity,
            bias=binc[:, bcol:bcol + 1]), ss)
    if rope_pair:
        cosb, sinb = tiles["cosb"], tiles["sinb"]
        r, u = tiles["rsc"], tiles["ru"]
        p.wait("vector", ss)
        p.op("vector", lambda E: E.tensor_tensor(r[:, 0:T], t12[1][:, :], sinb[:, :], ALU.mult), sv)
        p.op("vector", lambda E: E.tensor_tensor(u[:, 0:T], t12[0][:, :], sinb[:, :], ALU.mult), sv)
        p.op("vector", lambda E: E.tensor_tensor(t12[0][:, :], t12[0][:, :], cosb[:, :], ALU.mult), sv)
        p.op("vector", lambda E: E.tensor_tensor(t12[1][:, :], t12[1][:, :], cosb[:, :], ALU.mult), sv)
        p.op("vector", lambda E: E.tensor_tensor(t12[0][:, :], t12[0][:, :], r[:, 0:T], ALU.subtract), sv)
        p.op("vector", lambda E: E.tensor_tensor(t12[1][:, :], t12[1][:, :], u[:, 0:T], ALU.add), sv)
    fuse(p)
    p.marks["qkv_psum_free"] = p.val(sv)


def build_kva(fctx=None, pre=""):
    nc = fctx.nc if fctx else _mk_nc()
    xn_h = _io(fctx, nc, pre + "xn", [D, T], BF16, "ExternalInput")
    winT_h = _io(fctx, nc, pre + "winT", [D, 3 * D], F32, "ExternalInput")
    binc_h = _io(fctx, nc, pre + "binc", [128, 48], F32, "ExternalInput")
    cos_h = _io(fctx, nc, pre + "cosr", [512, T], F32, "ExternalInput")
    sin_h = _io(fctx, nc, pre + "sinr", [512, T], F32, "ExternalInput")
    consts = _io(fctx, nc, pre + "consts", [128, 8], F32, "ExternalInput")
    onesc_h = _io(fctx, nc, pre + "onesc", [128, 1], F32, "ExternalInput")
    onesr_h = _io(fctx, nc, pre + "onesr", [1, 128], F32, "ExternalInput")
    ap_h = _io(fctx, nc, pre + "apart", [128, 16], F32, "ExternalOutput")
    es = ExitStack()
    with es:
        xn = es.enter_context(nc.sbuf_tensor(pre + "s_xn", [128, 16 * T], BF16))
        kt = es.enter_context(nc.sbuf_tensor(pre + "kt", [128, 16 * T], BF16))
        wb = es.enter_context(nc.sbuf_tensor(pre + "wb", [128, 2 * 2048], BF16))
        t1 = es.enter_context(nc.sbuf_tensor(pre + "t1", [128, T], F32))
        t2 = es.enter_context(nc.sbuf_tensor(pre + "t2", [128, T], F32))
        rsc = es.enter_context(nc.sbuf_tensor(pre + "rsc", [128, T], F32))
        cosb = es.enter_context(nc.sbuf_tensor(pre + "cosb", [128, T], F32))
        sinb = es.enter_context(nc.sbuf_tensor(pre + "sinb", [128, T], F32))
        binc = es.enter_context(nc.sbuf_tensor(pre + "s_binc", [128, 48], F32))
        ct = es.enter_context(nc.sbuf_tensor(pre + "ct", [128, 8], F32))
        onesc = es.enter_context(nc.sbuf_tensor(pre + "s_onesc", [128, 1], F32))
        onesr = es.enter_context(nc.sbuf_tensor(pre + "s_onesr", [1, 128], F32))
        srow = es.enter_context(nc.sbuf_tensor(pre + "srow", [1, T], F32))
        rnb = es.enter_context(nc.sbuf_tensor(pre + "rnb", [128, T], F32))
        apt = es.enter_context(nc.sbuf_tensor(pre + "apt", [128, 16], F32))
        sq = es.enter_context(nc.sbuf_tensor(pre + "sq", [128, T], F32))
        ps0 = es.enter_context(nc.psum_tensor(pre + "ps0", [128, T], F32))
        ps1 = es.enter_context(nc.psum_tensor(pre + "ps1", [128, T], F32))
        S = fctx.S if fctx else _sems(es, nc, ALL_SEMS)
        sd, sg, so, sv, ss, sm, se = (S[k] for k in ["sd", "sg", "so", "sv", "ss", "sm", "se"])
        if fctx is None:
            p = Prog(nc)
            p.autosem = {"vector": sv, "scalar": ss}
        else:
            p = fctx.p
        tiles = dict(xn=xn, wb=wb, ps0=ps0, ps1=ps1, t1=t1, t2=t2, rsc=rsc,
                     ru=sq, cosb=cosb, sinb=sinb, binc=binc, winT_h=winT_h)

        p.dma("sync", binc[:, :], binc_h, sd)
        p.dma("sync", ct[:, :], consts, sd)
        p.dma("sync", onesc[:, :], onesc_h, sd)
        p.dma("sync", onesr[:, :], onesr_h, sd)
        for c in range(16):
            p.dma("sync", xn[:, c * T:(c + 1) * T], xn_h[c * 128:(c + 1) * 128, :], sd)
        for pg in range(4):
            p.wait("sync", sv)
            p.dma("sync", cosb[:, :], cos_h[pg * 128:(pg + 1) * 128, :], sd)
            p.dma("sync", sinb[:, :], sin_h[pg * 128:(pg + 1) * 128, :], sd)
            p.wait("vector", sd)
            p.wait("vector", sg)

            def fuse(p, pg=pg):
                p.wait("vector", S["ss"])
                p.op("vector", lambda E, pg=pg: E.tensor_copy(
                    kt[:, pg * T:(pg + 1) * T], t1[:, :]), sv, 1)
                p.op("vector", lambda E, pg=pg: E.tensor_copy(
                    kt[:, (pg + 4) * T:(pg + 5) * T], t2[:, :]), sv, 1)
            _emit_qkv_chunk(p, tiles, [pg, pg + 4], D, True, fuse, S)
        for kc in range(8, 16):
            def fuse(p, kc=kc):
                p.wait("vector", S["ss"])
                p.op("vector", lambda E, kc=kc: E.tensor_copy(
                    kt[:, kc * T:(kc + 1) * T], t1[:, :]), sv, 1)
            _emit_qkv_chunk(p, tiles, [kc], D, False, fuse, S)
        p.wait("scalar", sv)
        for c in range(16):
            p.wait("scalar", sm, p.val(sm))
            p.op("scalar", lambda E, c=c: E.activation(
                sq[:, :], kt[:, c * T:(c + 1) * T], AF.Square), ss)
            p.wait("tensor", ss)
            for tt in range(4):
                p.op("tensor", lambda E, c=c, tt=tt: E.matmul(
                    ps0[0:1, tt * 512:(tt + 1) * 512], onesc[:, :],
                    sq[:, tt * 512:(tt + 1) * 512],
                    start=(c == 0), stop=(c == 15)), sm if tt == 3 else None)
        p.wait("vector", sm)
        p.op("vector", lambda E: E.tensor_scalar(
            srow[0:1, :], ps0[0:1, :], 1e-24, None, ALU.max), sv)
        p.wait("scalar", sv)
        p.op("scalar", lambda E: E.activation(srow[0:1, :], srow[0:1, :], AF.Ln), ss)
        p.wait("scalar", ss)
        p.op("scalar", lambda E: E.activation(srow[0:1, :], srow[0:1, :], AF.Exp, scale=-0.5), ss)
        p.wait("tensor", ss)
        for tt in range(4):
            p.op("tensor", lambda E, tt=tt: E.matmul(
                ps1[:, tt * 512:(tt + 1) * 512], onesr[:, :],
                srow[0:1, tt * 512:(tt + 1) * 512], start=True, stop=True),
                sm if tt == 3 else None)
        p.wait("vector", sm)
        p.op("vector", lambda E: E.tensor_copy(rnb[:, :], ps1[:, :]), sv)
        p.marks["qkv_psum_free"] = p.val(sv)
        for ec in range(16):
            def fuse(p, ec=ec):
                p.wait("vector", S["ss"])
                p.op("vector", lambda E, ec=ec: E.tensor_tensor(
                    rsc[:, 0:T], kt[:, ec * T:(ec + 1) * T], t1[:, :], ALU.mult), sv)
                p.op("vector", lambda E: E.tensor_tensor(
                    rsc[:, 0:T], rsc[:, 0:T], rnb[:, :], ALU.mult), sv)
                p.op("vector", lambda E, ec=ec: E.reduce_sum(
                    apt[:, ec:ec + 1], rsc[:, 0:T], mybir.AxisListType.X), sv)
            _emit_qkv_chunk(p, tiles, [ec], 2 * D, False, fuse, S)
        p.wait("sync", sv)
        p.dma("sync", ap_h, apt[:, :], so)
        p.wait("sync", so)
        p.wait("sync", sd)
        p.wait("sync", sg)
        if fctx is None:
            block = es.enter_context(nc.Block())
            p.run(block)
        else:
            _barrier(p, S)
    return nc



def build_qattn(fctx=None, pre=""):
    nc = fctx.nc if fctx else _mk_nc()
    xn_h = _io(fctx, nc, pre + "xn", [D, T], BF16, "ExternalInput")
    x0_h = _io(fctx, nc, pre + "x0", [D, T], F32, "ExternalInput")
    winT_h = _io(fctx, nc, pre + "winT", [D, 3 * D], F32, "ExternalInput")
    binc_h = _io(fctx, nc, pre + "binc", [128, 48], F32, "ExternalInput")
    cos_h = _io(fctx, nc, pre + "cosr", [512, T], F32, "ExternalInput")
    sin_h = _io(fctx, nc, pre + "sinr", [512, T], F32, "ExternalInput")
    woutT_h = _io(fctx, nc, pre + "woutT", [D, D], F32, "ExternalInput")
    boutc_h = _io(fctx, nc, pre + "boutc", [128, 16], F32, "ExternalInput")
    am_h = _io(fctx, nc, pre + "amine", [128, 16], F32, "ExternalInput")
    ap2_h = _io(fctx, nc, pre + "apartner", [128, 16], F32, "ExternalInput")
    px1_h = _io(fctx, nc, pre + "px1", [D, T], F32, "ExternalOutput")
    es = ExitStack()
    with es:
        xn = es.enter_context(nc.sbuf_tensor(pre + "s_xn", [128, 16 * T], BF16))
        yt = es.enter_context(nc.sbuf_tensor(pre + "yt", [128, 16 * T], BF16))
        wb = es.enter_context(nc.sbuf_tensor(pre + "wb", [128, 2 * 2048], BF16))
        wob = es.enter_context(nc.sbuf_tensor(pre + "wob", [128, 2048], BF16))
        t1 = es.enter_context(nc.sbuf_tensor(pre + "t1", [128, T], F32))
        t2 = es.enter_context(nc.sbuf_tensor(pre + "t2", [128, T], F32))
        rsc = es.enter_context(nc.sbuf_tensor(pre + "rsc", [128, T], F32))
        ru = es.enter_context(nc.sbuf_tensor(pre + "ru", [128, T], F32))
        cosb = es.enter_context(nc.sbuf_tensor(pre + "cosb", [128, T], F32))
        sinb = es.enter_context(nc.sbuf_tensor(pre + "sinb", [128, T], F32))
        binc = es.enter_context(nc.sbuf_tensor(pre + "s_binc", [128, 48], F32))
        boutc = es.enter_context(nc.sbuf_tensor(pre + "s_boutc", [128, 16], F32))
        at_ = es.enter_context(nc.sbuf_tensor(pre + "at_", [128, 16], F32))
        ap2 = es.enter_context(nc.sbuf_tensor(pre + "ap2", [128, 16], F32))
        x0b = es.enter_context(nc.sbuf_tensor(pre + "x0b", [128, T], F32))
        pxb = es.enter_context(nc.sbuf_tensor(pre + "pxb", [128, T], F32))
        sq = es.enter_context(nc.sbuf_tensor(pre + "sq", [128, 1], F32))
        ps0 = es.enter_context(nc.psum_tensor(pre + "ps0", [128, T], F32))
        ps1 = es.enter_context(nc.psum_tensor(pre + "ps1", [128, T], F32))
        S = fctx.S if fctx else _sems(es, nc, ALL_SEMS)
        sd, sg, so, sv, ss, sm, se = (S[k] for k in ["sd", "sg", "so", "sv", "ss", "sm", "se"])
        if fctx is None:
            p = Prog(nc)
            p.autosem = {"vector": sv, "scalar": ss}
        else:
            p = fctx.p
        tiles = dict(xn=xn, wb=wb, ps0=ps0, ps1=ps1, t1=t1, t2=t2, rsc=rsc,
                     ru=ru, cosb=cosb, sinb=sinb, binc=binc, winT_h=winT_h)

        p.dma("sync", binc[:, :], binc_h, sd)
        p.dma("sync", boutc[:, :], boutc_h, sd)
        p.dma("sync", at_[:, :], am_h, sd)
        p.dma("sync", ap2[:, :], ap2_h, sd)
        for c in range(16):
            p.dma("sync", xn[:, c * T:(c + 1) * T], xn_h[c * 128:(c + 1) * 128, :], sd)
        p.wait("vector", sd)
        p.wait("vector", sg)
        p.op("vector", lambda E: E.tensor_tensor(at_[:, :], at_[:, :], ap2[:, :], ALU.add), sv)
        p.wait("vector", sv)

        def mk_fuse(chunks):
            def fuse(p):
                p.wait("vector", S["ss"])
                for j, qc in enumerate(chunks):
                    p.op("vector", lambda E, j=j, qc=qc: E.tensor_scalar(
                        yt[:, qc * T:(qc + 1) * T], (t1 if j == 0 else t2)[:, :],
                        at_[:, qc:qc + 1], None, ALU.mult), sv)
            return fuse
        for pg in range(4):
            p.wait("sync", sv)
            p.dma("sync", cosb[:, :], cos_h[pg * 128:(pg + 1) * 128, :], sd)
            p.dma("sync", sinb[:, :], sin_h[pg * 128:(pg + 1) * 128, :], sd)
            p.wait("vector", sd)
            p.wait("vector", sg)
            _emit_qkv_chunk(p, tiles, [pg, pg + 4], 0, True, mk_fuse([pg, pg + 4]), S)
        for qc in range(8, 16):
            _emit_qkv_chunk(p, tiles, [qc], 0, False, mk_fuse([qc]), S)

        for dc in range(16):
            p.wait("gpsimd", sm)
            p.dma("gpsimd", wob[:, :],
                  woutT_h.rearrange("(ec k) d -> k ec d", k=128)[:, :, dc * 128:(dc + 1) * 128], sg)
            p.wait("sync", sv, p.val(sv))
            p.dma("sync", x0b[:, :], x0_h[dc * 128:(dc + 1) * 128, :], sd)
            p.wait("tensor", sd)
            p.wait("tensor", sg)
            p.wait("tensor", ss, p.marks.get("qo_psum_free", 0))
            for tt in range(4):
                for ec in range(16):
                    p.op("tensor", lambda E, dc=dc, tt=tt, ec=ec: E.matmul(
                        ps0[:, tt * 512:(tt + 1) * 512],
                        wob[:, ec * 128:(ec + 1) * 128],
                        yt[:, ec * T + tt * 512:ec * T + (tt + 1) * 512],
                        start=(ec == 0), stop=(ec == 15)),
                        sm if (ec == 15 and tt == 3) else None)
            p.wait("scalar", sm)
            p.wait("scalar", sv)
            p.op("scalar", lambda E, dc=dc: E.activation(
                t1[:, :], ps0[:, :], AF.Identity, bias=boutc[:, dc:dc + 1]), ss)
            p.marks["qo_psum_free"] = p.val(ss)
            p.wait("vector", ss)
            p.op("vector", lambda E, dc=dc: E.tensor_tensor(
                t2[:, :], t1[:, :], xn[:, dc * T:(dc + 1) * T], ALU.add), sv)
            if dc >= 1:
                p.wait("vector", so, 16 * dc)
            p.op("vector", lambda E, dc=dc: E.tensor_tensor(
                pxb[:, :], t2[:, :], x0b[:, :], ALU.add), sv)
            p.wait("sync", sv)
            p.dma("sync", px1_h[dc * 128:(dc + 1) * 128, :], pxb[:, :], so)
        p.wait("sync", so)
        p.wait("sync", sd)
        p.wait("sync", sg)
        if fctx is None:
            block = es.enter_context(nc.Block())
            p.run(block)
        else:
            _barrier(p, S)
    return nc



def emit_decode(p, S, src, scl, out_lo, out_hi, scr, ct, half=False):
    sv, ss = S["sv"], S["ss"]
    i4l, i4h = scr["i4l"], scr["i4h"]
    sgl, sgh, q1, q2 = scr["sgl"], scr["sgh"], scr["q1"], scr["q2"]
    b = 0.5 if half else 1.0
    p.op("vector", lambda E: E.tensor_scalar(i4l, src, 15, None, ALU.bitwise_and), sv)
    p.op("vector", lambda E: E.tensor_scalar(i4h, src, 4, None, ALU.logical_shift_right), sv)
    p.op("vector", lambda E: E.tensor_scalar(sgl, i4l, 8, None, ALU.is_lt), sv)
    p.op("vector", lambda E: E.tensor_scalar(sgl, sgl, 2.0, -1.0, ALU.mult, ALU.add), sv)
    p.op("vector", lambda E: E.tensor_scalar(sgh, i4h, 8, None, ALU.is_lt), sv)
    p.op("vector", lambda E: E.tensor_scalar(sgh, sgh, 2.0, -1.0, ALU.mult, ALU.add), sv)
    for j4, sg, out in ((i4l, sgl, out_lo), (i4h, sgh, out_hi)):
        p.op("vector", lambda E, j4=j4: E.tensor_scalar(j4, j4, 7, None, ALU.bitwise_and), sv)
        p.op("vector", lambda E, j4=j4: E.tensor_scalar(q1, j4, 0.5 * b, None, ALU.mult), sv)
        p.op("vector", lambda E, j4=j4: E.tensor_scalar(q2, j4, b, -2.0 * b, ALU.mult, ALU.add), sv)
        p.op("vector", lambda E: E.tensor_tensor(q1, q1, q2, ALU.max), sv)
        p.op("vector", lambda E, j4=j4: E.tensor_scalar(q2, j4, 2.0 * b, -8.0 * b, ALU.mult, ALU.add), sv)
        p.op("vector", lambda E: E.tensor_tensor(q1, q1, q2, ALU.max), sv)
        p.op("vector", lambda E, sg=sg: E.tensor_tensor(q1, q1, sg, ALU.mult), sv)
        p.op("vector", lambda E, out=out: E.tensor_tensor(out, q1, scl, ALU.mult), sv)


def _decode_scratch(es, nc, W, tag=""):
    return dict(
        i4l=es.enter_context(nc.sbuf_tensor(f"i4l{tag}", [128, W], I32))[:, :],
        i4h=es.enter_context(nc.sbuf_tensor(f"i4h{tag}", [128, W], I32))[:, :],
        sgl=es.enter_context(nc.sbuf_tensor(f"sgl{tag}", [128, W], BF16))[:, :],
        sgh=es.enter_context(nc.sbuf_tensor(f"sgh{tag}", [128, W], BF16))[:, :],
        q1=es.enter_context(nc.sbuf_tensor(f"q1{tag}", [128, W], BF16))[:, :],
        q2=es.enter_context(nc.sbuf_tensor(f"q2{tag}", [128, W], BF16))[:, :],
    )


def _scr_view(scr, W):
    return {k: v[:, 0:W] for k, v in scr.items()}





def _gelu_evac(p, S, ud, psg, psu, eg, e2, g3, th_, tp1, ue, intb, inter_h,
               ct, siP):
    irow, th, pp, mmval = ud
    sv, ss, sm = S["sv"], S["ss"], S["sm"]
    ib = (irow // 128 * 2 + th) % 2
    p.wait("scalar", sm, mmval)
    p.wait("vector", siP[ib])
    for tq in range(2):
        sl = slice(pp * 1024 + tq * 512, pp * 1024 + (tq + 1) * 512)
        p.wait("scalar", sv, p.val(sv))
        p.op("scalar", lambda E, sl=sl: E.activation(eg[:, :], psg[:, sl], AF.Copy), ss)
        p.op("scalar", lambda E, sl=sl: E.activation(e2[:, :], psg[:, sl], AF.Square), ss)
        p.wait("vector", ss)
        p.op("vector", lambda E: E.tensor_tensor(g3[:, :], e2[:, :], eg[:, :], ALU.mult), sv)
        p.op("vector", lambda E: E.scalar_tensor_tensor(
            g3[:, :], g3[:, :], 0.044715, eg[:, :], ALU.mult, ALU.add), sv)
        p.wait("scalar", sv)
        p.op("scalar", lambda E: E.activation(
            th_[:, :], g3[:, :], AF.Tanh, scale=0.7978845608028654), ss)
        p.op("scalar", lambda E: E.activation(
            tp1[:, :], th_[:, :], AF.Identity, bias=ct[:, C_ONE:C_ONE + 1]), ss)
        p.op("scalar", lambda E, sl=sl: E.activation(ue[:, :], psu[:, sl], AF.Copy), ss)
        p.wait("vector", ss)
        p.op("vector", lambda E: E.tensor_tensor(g3[:, :], eg[:, :], tp1[:, :], ALU.mult), sv)
        p.op("vector", lambda E, ib=ib, tq=tq: E.tensor_tensor(
            intb[:, ib * 1024 + tq * 512:ib * 1024 + (tq + 1) * 512],
            g3[:, :], ue[:, :], ALU.mult), sv)
    p.marks[f"gu_ps_{pp}"] = p.val(ss)
    p.wait("sync", sv)
    p.dma("sync", inter_h[irow:irow + 128, th * 1024:(th + 1) * 1024],
          intb[:, ib * 1024:(ib + 1) * 1024], siP[ib])


def build_ffn0a(fctx=None, pre=""):
    nc = fctx.nc if fctx else _mk_nc()
    xn_h = _io(fctx, nc, pre + "xn", [D, T], BF16, "ExternalInput")
    gpk_h = _io(fctx, nc, pre + "gpk", [1024, NF], I32, "ExternalInput")
    gsc_h = _io(fctx, nc, pre + "gsc", [64, NF], F32, "ExternalInput")
    upk_h = _io(fctx, nc, pre + "upk", [1024, NF], I32, "ExternalInput")
    usc_h = _io(fctx, nc, pre + "usc", [64, NF], F32, "ExternalInput")
    dpk_h = _io(fctx, nc, pre + "dpk", [NF // 2, D], I32, "ExternalInput")
    dsc_h = _io(fctx, nc, pre + "dsc", [512, D], F32, "ExternalInput")
    consts = _io(fctx, nc, pre + "consts", [128, 8], F32, "ExternalInput")
    ddec_h = _io(fctx, nc, pre + "ddec", [NF, D], BF16, "ExternalOutput")
    inter_h = _io(fctx, nc, pre + "inter", [NF, T], BF16, "ExternalOutput")
    es = ExitStack()
    with es:
        xn = es.enter_context(nc.sbuf_tensor(pre + "s_xn", [128, 16 * T], BF16))
        ct = es.enter_context(nc.sbuf_tensor(pre + "ct", [128, 8], F32))
        gpkb = es.enter_context(nc.sbuf_tensor(pre + "gpkb", [128, 4096], I32))
        upkb = es.enter_context(nc.sbuf_tensor(pre + "upkb", [128, 4096], I32))
        gscb = es.enter_context(nc.sbuf_tensor(pre + "gscb", [128, 4096], BF16))
        uscb = es.enter_context(nc.sbuf_tensor(pre + "uscb", [128, 4096], BF16))
        glo = es.enter_context(nc.sbuf_tensor(pre + "glo", [128, 4096], BF16))
        ghi = es.enter_context(nc.sbuf_tensor(pre + "ghi", [128, 4096], BF16))
        ulo = es.enter_context(nc.sbuf_tensor(pre + "ulo", [128, 4096], BF16))
        uhi = es.enter_context(nc.sbuf_tensor(pre + "uhi", [128, 4096], BF16))
        scr = _decode_scratch(es, nc, 2048, pre)
        eg = es.enter_context(nc.sbuf_tensor(pre + "eg", [128, 512], F32))
        e2 = es.enter_context(nc.sbuf_tensor(pre + "e2", [128, 512], F32))
        g3 = es.enter_context(nc.sbuf_tensor(pre + "g3", [128, 512], F32))
        th_ = es.enter_context(nc.sbuf_tensor(pre + "th", [128, 512], F32))
        tp1 = es.enter_context(nc.sbuf_tensor(pre + "tp1", [128, 512], F32))
        ue = es.enter_context(nc.sbuf_tensor(pre + "ue", [128, 512], BF16))
        intb = es.enter_context(nc.sbuf_tensor(pre + "intb", [128, 2 * T], BF16))
        psg = es.enter_context(nc.psum_tensor(pre + "psg", [128, T], F32))
        psu = es.enter_context(nc.psum_tensor(pre + "psu", [128, T], F32))
        S = fctx.S if fctx else _sems(es, nc, ALL_SEMS)
        (sd, sg, so, sv, ss, sm, se, sk0, sk1, sc0, sc1, st0, st1, si0, si1) = (
            S[k] for k in ["sd", "sg", "so", "sv", "ss", "sm", "se",
                           "sk0", "sk1", "sc0", "sc1", "st0", "st1", "si0", "si1"])
        skP, scP, stP, siP = [sk0, sk1], [sc0, sc1], [st0, st1], [si0, si1]
        if fctx is None:
            p = Prog(nc)
            p.autosem = {"vector": sv, "scalar": ss}
        else:
            p = fctx.p

        p.dma("sync", ct[:, :], consts, sd)
        for c in range(16):
            p.dma("sync", xn[:, c * T:(c + 1) * T], xn_h[c * 128:(c + 1) * 128, :], sd)

        pkb, sclb, olo, ohi = gpkb, gscb, glo, ghi
        for c in range(64):
            hb = c % 2
            if c >= 2:
                p.wait("sync", sv, p.val(sv))
                p.wait("gpsimd", sv, p.val(sv))
            p.dma("sync", pkb[:, hb * D:(hb + 1) * D],
                  dpk_h[c * 128:(c + 1) * 128, :], skP[hb])
            p.dma("gpsimd", sclb[:, hb * D:(hb + 1) * D],
                  dsc_h[c * 8:(c + 1) * 8, :].unsqueeze(1).broadcast_to([8, 16, D]), scP[hb])
            p.wait("vector", sd)
            p.wait("vector", skP[hb])
            p.wait("vector", scP[hb])
            if c >= 2:
                p.wait("vector", stP[hb])
            emit_decode(p, S, pkb[:, hb * D:(hb + 1) * D], sclb[:, hb * D:(hb + 1) * D],
                        olo[:, hb * D:(hb + 1) * D], ohi[:, hb * D:(hb + 1) * D],
                        scr, ct, half=True)
            p.wait("sync", sv)
            p.dma("sync", ddec_h[c * 128:(c + 1) * 128, :], olo[:, hb * D:(hb + 1) * D], stP[hb])
            p.dma("sync", ddec_h[NF // 2 + c * 128:NF // 2 + (c + 1) * 128, :],
                  ohi[:, hb * D:(hb + 1) * D], stP[hb])

        p.wait("sync", sv)
        p.wait("gpsimd", sv)
        p.wait("vector", st0)
        p.wait("vector", st1)
        ldP = [[sk0, sc0], [sk1, sc1]]
        unit = 0
        UD = {}
        for nsl in range(64):
            par = nsl % 2
            n0 = nsl * 256
            if nsl >= 2:
                p.wait("sync", sv, p.marks.get(f"gu_dec_{par}", 0))
                p.wait("gpsimd", sv, p.marks.get(f"gu_dec_{par}", 0))
            p.dma("sync", gpkb[:, par * 2048:(par + 1) * 2048],
                  gpk_h.rearrange("(kc k) n -> k kc n", k=128)[:, :, n0:n0 + 256], ldP[par][0])
            p.dma("sync", upkb[:, par * 2048:(par + 1) * 2048],
                  upk_h.rearrange("(kc k) n -> k kc n", k=128)[:, :, n0:n0 + 256], ldP[par][0])
            for kc in range(8):
                p.dma("gpsimd", gscb[:, par * 2048 + kc * 256:par * 2048 + (kc + 1) * 256],
                      gsc_h[kc * 8:(kc + 1) * 8, n0:n0 + 256]
                      .unsqueeze(1).broadcast_to([8, 16, 256]), ldP[par][1])
                p.dma("gpsimd", uscb[:, par * 2048 + kc * 256:par * 2048 + (kc + 1) * 256],
                      usc_h[kc * 8:(kc + 1) * 8, n0:n0 + 256]
                      .unsqueeze(1).broadcast_to([8, 16, 256]), ldP[par][1])
            p.wait("vector", ldP[par][0])
            p.wait("vector", ldP[par][1])
            if nsl >= 2:
                p.wait("vector", sm, p.marks.get(f"gu_mm_{par}", 0))
            W0 = par * 2048
            emit_decode(p, S, gpkb[:, W0:W0 + 2048], gscb[:, W0:W0 + 2048],
                        glo[:, W0:W0 + 2048], ghi[:, W0:W0 + 2048], scr, ct)
            emit_decode(p, S, upkb[:, W0:W0 + 2048], uscb[:, W0:W0 + 2048],
                        ulo[:, W0:W0 + 2048], uhi[:, W0:W0 + 2048], scr, ct)
            p.marks[f"gu_dec_{par}"] = p.val(sv)
            dec_mark = p.val(sv)
            for nsub in range(2):
                nc0 = W0 + nsub * 128
                irow = n0 + nsub * 128
                for th in range(2):
                    pp = unit % 2
                    p.wait("tensor", sv, dec_mark)
                    p.wait("tensor", ss, p.marks.get(f"gu_ps_{pp}", 0))
                    for mi, (ps_, lo, hi) in enumerate(((psg, glo, ghi), (psu, ulo, uhi))):
                        pbase = pp * 1024
                        for tq in range(2):
                            tt = th * 2 + tq
                            for kc in range(8):
                                p.op("tensor", lambda E, ps_=ps_, lo=lo, tq=tq, tt=tt, kc=kc, nc0=nc0, pbase=pbase: E.matmul(
                                    ps_[:, pbase + tq * 512:pbase + (tq + 1) * 512],
                                    lo[:, nc0 + kc * 256:nc0 + kc * 256 + 128],
                                    xn[:, kc * T + tt * 512:kc * T + (tt + 1) * 512],
                                    start=(kc == 0), stop=False))
                            for kc in range(8):
                                p.op("tensor", lambda E, ps_=ps_, hi=hi, tq=tq, tt=tt, kc=kc, nc0=nc0, pbase=pbase: E.matmul(
                                    ps_[:, pbase + tq * 512:pbase + (tq + 1) * 512],
                                    hi[:, nc0 + kc * 256:nc0 + kc * 256 + 128],
                                    xn[:, (8 + kc) * T + tt * 512:(8 + kc) * T + (tt + 1) * 512],
                                    start=False, stop=(kc == 7)),
                                    sm if (kc == 7) else None)
                    UD[unit] = (irow, th, pp, p.val(sm))
                    if unit >= 1:
                        _gelu_evac(p, S, UD[unit - 1], psg, psu, eg, e2, g3, th_,
                                   tp1, ue, intb, inter_h, ct, siP)
                    unit += 1
            p.marks[f"gu_mm_{par}"] = p.val(sm)
        _gelu_evac(p, S, UD[unit - 1], psg, psu, eg, e2, g3, th_, tp1, ue,
                   intb, inter_h, ct, siP)
        for s_ in (so, sk0, sk1, sc0, sc1, st0, st1, si0, si1, sd, sg):
            p.wait("sync", s_)
        p.wait("sync", sg)
        if fctx is None:
            block = es.enter_context(nc.Block())
            p.run(block)
        else:
            _barrier(p, S)
    return nc



def build_ffn0b(fctx=None, pre=""):
    nc = fctx.nc if fctx else _mk_nc()
    inter_h = _io(fctx, nc, pre + "inter", [NF, T], BF16, "ExternalInput")
    ddec_h = _io(fctx, nc, pre + "ddec", [NF, D], BF16, "ExternalInput")
    px1_h = _io(fctx, nc, pre + "px1", [D, T], F32, "ExternalInput")
    x1_h = _io(fctx, nc, pre + "x1", [D, T], F32, "ExternalOutput")
    es = ExitStack()
    with es:
        intt = es.enter_context(nc.sbuf_tensor(pre + "intt", [128, 128 * 512], BF16))
        ddb = es.enter_context(nc.sbuf_tensor(pre + "ddb", [128, 4 * 1024], BF16))
        pxb = es.enter_context(nc.sbuf_tensor(pre + "pxb", [128, 512], F32))
        t1 = es.enter_context(nc.sbuf_tensor(pre + "t1", [128, 512], F32))
        x1b = es.enter_context(nc.sbuf_tensor(pre + "x1b", [128, 2 * 512], F32))
        ps0 = es.enter_context(nc.psum_tensor(pre + "ps0", [128, T], F32))
        ps1 = es.enter_context(nc.psum_tensor(pre + "ps1", [128, T], F32))
        S = fctx.S if fctx else _sems(es, nc, ALL_SEMS)
        (sd, sg, sg1, so, so1, sdp, sv, ss, sm, se) = (
            S[k] for k in ["sd", "sg", "sg1", "so", "so1", "sdp", "sv", "ss", "sm", "se"])
        sgP = [sg, sg1, S["sc0"], S["sc1"]]
        soP = [so, so1]
        if fctx is None:
            p = Prog(nc)
            p.autosem = {"vector": sv, "scalar": ss}
        else:
            p = fctx.p

        def psv(hc8):
            ps_ = ps0 if hc8 < 4 else ps1
            return ps_[:, (hc8 % 4) * 512:(hc8 % 4 + 1) * 512]

        for tt in range(4):
            p.wait("sync", sm)
            for half in range(2):
                p.dma("sync", intt[:, half * 64 * 512:(half + 1) * 64 * 512],
                      inter_h.rearrange("(ncc k) t -> k ncc t", k=128)
                      [:, half * 64:(half + 1) * 64, tt * 512:(tt + 1) * 512], sd)
            for hg in range(2):
                p.wait("tensor", ss, p.marks.get("d2_psum_free", 0))
                p.wait("tensor", sd)
                NBUF = 4
                for ncc in range(128):
                    pb = ncc % NBUF
                    loads = [ncc] if ncc else list(range(min(NBUF - 1, 128)))
                    if ncc == 0:
                        for lnc in range(min(NBUF - 1, 128)):
                            lb = lnc % NBUF
                            p.wait("gpsimd", sm, p.val(sm))
                            p.dma("gpsimd", ddb[:, lb * 1024:(lb + 1) * 1024],
                                  ddec_h[lnc * 128:(lnc + 1) * 128,
                                         hg * 1024:(hg + 1) * 1024], sgP[lb])
                    lnc = ncc + NBUF - 1
                    if lnc < 128:
                        lb = lnc % NBUF
                        p.wait("gpsimd", sm, p.val(sm))
                        p.dma("gpsimd", ddb[:, lb * 1024:(lb + 1) * 1024],
                              ddec_h[lnc * 128:(lnc + 1) * 128,
                                     hg * 1024:(hg + 1) * 1024], sgP[lb])
                    p.wait("tensor", sgP[pb])
                    for hc8 in range(8):
                        p.op("tensor", lambda E, pb=pb, hc8=hc8, ncc=ncc: E.matmul(
                            psv(hc8), ddb[:, pb * 1024 + hc8 * 128:pb * 1024 + (hc8 + 1) * 128],
                            intt[:, ncc * 512:(ncc + 1) * 512],
                            start=(ncc == 0), stop=(ncc == 127)),
                            sm if hc8 == 7 else None)
                p.wait("scalar", sm)
                for hc8 in range(8):
                    hc = hg * 8 + hc8
                    i = tt * 16 + hc
                    p.wait("sync", sv, p.val(sv))
                    p.dma("sync", pxb[:, :],
                          px1_h[hc * 128:(hc + 1) * 128, tt * 512:(tt + 1) * 512], sdp)
                    p.wait("scalar", sv, p.val(sv))
                    p.op("scalar", lambda E, hc8=hc8: E.activation(
                        t1[:, :], psv(hc8), AF.Copy), ss)
                    if hc8 == 7:
                        p.marks["d2_psum_free"] = p.val(ss)
                    p.wait("vector", ss)
                    p.wait("vector", sdp)
                    b = i % 2
                    if i >= 2:
                        p.wait("vector", soP[b])
                    p.op("vector", lambda E, b=b: E.tensor_tensor(
                        x1b[:, b * 512:(b + 1) * 512], t1[:, :], pxb[:, :], ALU.add), sv)
                    p.wait("sync", sv)
                    p.dma("sync", x1_h[hc * 128:(hc + 1) * 128, tt * 512:(tt + 1) * 512],
                          x1b[:, b * 512:(b + 1) * 512], soP[b])
        for s_ in (so, so1, sd, sg, sg1, sdp):
            p.wait("sync", s_)
        if fctx is None:
            block = es.enter_context(nc.Block())
            p.run(block)
        else:
            _barrier(p, S)
    return nc



def build_zmm(fctx=None, pre=""):
    nc = fctx.nc if fctx else _mk_nc()
    xn_h = _io(fctx, nc, pre + "xn2", [D, T], BF16, "ExternalInput")
    wsh_h = _io(fctx, nc, pre + "wshT", [D, D], F32, "ExternalInput")
    z_h = _io(fctx, nc, pre + "z", [D, T], BF16, "ExternalOutput")
    es = ExitStack()
    with es:
        xn = es.enter_context(nc.sbuf_tensor(pre + "s_xn", [128, 16 * T], BF16))
        wb = es.enter_context(nc.sbuf_tensor(pre + "wb", [128, 2 * 2048], BF16))
        zb = es.enter_context(nc.sbuf_tensor(pre + "zb", [128, 2 * T], BF16))
        ps0 = es.enter_context(nc.psum_tensor(pre + "ps0", [128, T], F32))
        ps1 = es.enter_context(nc.psum_tensor(pre + "ps1", [128, T], F32))
        S = fctx.S if fctx else _sems(es, nc, ALL_SEMS)
        (sd, sg, sg1, so, so1, sv, ss, sm, se) = (
            S[k] for k in ["sd", "sg", "sg1", "so", "so1", "sv", "ss", "sm", "se"])
        sgP, soP = [sg, sg1], [so, so1]
        if fctx is None:
            p = Prog(nc)
            p.autosem = {"vector": sv, "scalar": ss}
        else:
            p = fctx.p
        for c in range(16):
            p.dma("sync", xn[:, c * T:(c + 1) * T], xn_h[c * 128:(c + 1) * 128, :], sd)
        for ec in range(16):
            eb = ec % 2
            ps_ = ps0 if eb == 0 else ps1
            p.wait("gpsimd", sm, max(0, p.val(sm) - 1))
            p.dma("gpsimd", wb[:, eb * 2048:(eb + 1) * 2048],
                  wsh_h.rearrange("(dc k) e -> k dc e", k=128)[:, :, ec * 128:(ec + 1) * 128], sgP[eb])
            p.wait("tensor", sd)
            p.wait("tensor", sgP[eb])
            if ec >= 2:
                p.wait("tensor", sv, p.marks.get(f"z_psum_free_{ec % 2}", 0))
            for tt in range(4):
                for dc in range(16):
                    p.op("tensor", lambda E, ps_=ps_, eb=eb, tt=tt, dc=dc: E.matmul(
                        ps_[:, tt * 512:(tt + 1) * 512],
                        wb[:, eb * 2048 + dc * 128:eb * 2048 + (dc + 1) * 128],
                        xn[:, dc * T + tt * 512:dc * T + (tt + 1) * 512],
                        start=(dc == 0), stop=(dc == 15)),
                        sm if (dc == 15 and tt == 3) else None)
            p.wait("vector", sm)
            if ec >= 2:
                p.wait("vector", soP[eb])
            p.op("vector", lambda E, ps_=ps_, eb=eb: E.tensor_copy(
                zb[:, eb * T:(eb + 1) * T], ps_[:, :]), sv)
            p.marks[f"z_psum_free_{eb}"] = p.val(sv)
            p.wait("sync", sv)
            p.dma("sync", z_h[ec * 128:(ec + 1) * 128, :], zb[:, eb * T:(eb + 1) * T], soP[eb])
        for s_ in (so, so1, sd, sg, sg1):
            p.wait("sync", s_)
        p.wait("sync", sg)
        if fctx is None:
            block = es.enter_context(nc.Block())
            p.run(block)
        else:
            _barrier(p, S)
    return nc



def build_l1attn(fctx=None, pre=""):
    nc = fctx.nc if fctx else _mk_nc()
    xn_h = _io(fctx, nc, pre + "xn2", [D, T], BF16, "ExternalInput")
    zp_h = _io(fctx, nc, pre + "zp", [D, T], BF16, "ExternalInput")
    gatec_h = _io(fctx, nc, pre + "gatec", [128, 16], F32, "ExternalInput")
    lngc_h = _io(fctx, nc, pre + "lngc", [128, 16], F32, "ExternalInput")
    lnbc_h = _io(fctx, nc, pre + "lnbc", [128, 16], F32, "ExternalInput")
    wkvT_h = _io(fctx, nc, pre + "wkvT", [D, 128], F32, "ExternalInput")
    tdc_h = _io(fctx, nc, pre + "tdc", [64, 1], F32, "ExternalInput")
    tfc_h = _io(fctx, nc, pre + "tfc", [64, 1], F32, "ExternalInput")
    consts = _io(fctx, nc, pre + "consts", [128, 8], F32, "ExternalInput")
    onesc_h = _io(fctx, nc, pre + "onesc", [128, 1], F32, "ExternalInput")
    onescb_h = _io(fctx, nc, pre + "onescb", [128, 1], BF16, "ExternalInput")
    onesr_h = _io(fctx, nc, pre + "onesr", [1, 128], F32, "ExternalInput")
    st_h = _io(fctx, nc, pre + "states", [64, T], F32, "ExternalOutput")
    sf_h = _io(fctx, nc, pre + "sfin", [64, 1], F32, "ExternalOutput")
    es = ExitStack()
    with es:
        xst = es.enter_context(nc.sbuf_tensor(pre + "xst", [128, 16 * T], BF16))
        xb = es.enter_context(nc.sbuf_tensor(pre + "xb", [128, T], BF16))
        zb = es.enter_context(nc.sbuf_tensor(pre + "zb", [128, T], BF16))
        t1 = es.enter_context(nc.sbuf_tensor(pre + "t1", [128, T], F32))
        sq = es.enter_context(nc.sbuf_tensor(pre + "sq", [128, T], F32))
        gt = es.enter_context(nc.sbuf_tensor(pre + "gt", [128, 16], F32))
        gmt = es.enter_context(nc.sbuf_tensor(pre + "gmt", [128, 16], F32))
        lng = es.enter_context(nc.sbuf_tensor(pre + "lng", [128, 16], F32))
        lnb = es.enter_context(nc.sbuf_tensor(pre + "lnb", [128, 16], F32))
        ct = es.enter_context(nc.sbuf_tensor(pre + "ct", [128, 8], F32))
        onesc = es.enter_context(nc.sbuf_tensor(pre + "s_onesc", [128, 1], F32))
        onescb = es.enter_context(nc.sbuf_tensor(pre + "s_onescb", [128, 1], BF16))
        onesr = es.enter_context(nc.sbuf_tensor(pre + "s_onesr", [1, 128], F32))
        srow = es.enter_context(nc.sbuf_tensor(pre + "srow", [1, 4 * T], F32))
        bc = es.enter_context(nc.sbuf_tensor(pre + "bc", [128, 2 * T], F32))
        wkv_w = es.enter_context(nc.sbuf_tensor(pre + "wkv_w", [128, 16 * 128], BF16))
        xnrm = es.enter_context(nc.sbuf_tensor(pre + "xnrm", [128, 16 * 512], BF16))
        kwk = es.enter_context(nc.sbuf_tensor(pre + "kwk", [64, 3 * 512], F32))
        wkvt = es.enter_context(nc.sbuf_tensor(pre + "wkvt", [64, T], F32))
        stt = es.enter_context(nc.sbuf_tensor(pre + "s_stt", [64, T], F32))
        tdc = es.enter_context(nc.sbuf_tensor(pre + "s_tdc", [64, 1], F32))
        negu = es.enter_context(nc.sbuf_tensor(pre + "negu", [64, 1], F32))
        wcol = es.enter_context(nc.sbuf_tensor(pre + "wcol", [64, 1], F32))
        ps = es.enter_context(nc.psum_tensor(pre + "ps", [1, T], F32))
        psb = es.enter_context(nc.psum_tensor(pre + "psb", [128, 512], F32))
        pskv = es.enter_context(nc.psum_tensor(pre + "pskv", [128, 512], F32))
        S = fctx.S if fctx else _sems(es, nc, ALL_SEMS)
        sd, sg, so, sv, ss, sm, se = (S[k] for k in ["sd", "sg", "so", "sv", "ss", "sm", "se"])
        if fctx is None:
            p = Prog(nc)
            p.autosem = {"vector": sv, "scalar": ss}
        else:
            p = fctx.p

        p.dma("sync", ct[:, :], consts, sd)
        p.dma("sync", gt[:, :], gatec_h, sd)
        p.dma("sync", lng[:, :], lngc_h, sd)
        p.dma("sync", lnb[:, :], lnbc_h, sd)
        p.dma("sync", onesc[:, :], onesc_h, sd)
        p.dma("sync", onescb[:, :], onescb_h, sd)
        p.dma("sync", onesr[:, :], onesr_h, sd)
        p.dma("sync", tdc[:, :], tdc_h, sd)
        p.dma("sync", negu[:, :], tfc_h, sd)
        p.dma("gpsimd", wkv_w[:, :],
              wkvT_h.rearrange("(dc k) s -> k dc s", k=128), sg)
        p.wait("scalar", sd)
        p.wait("scalar", sg)
        p.op("scalar", lambda E: E.activation(gt[:, :], gt[:, :], AF.Sigmoid), ss)
        p.op("scalar", lambda E: E.activation(negu[:, :], negu[:, :], AF.Exp), ss)
        p.op("scalar", lambda E: E.activation(wcol[:, :], tdc[:, :], AF.Exp), ss)
        p.wait("vector", ss)
        p.op("vector", lambda E: E.tensor_scalar(gmt[:, :], gt[:, :], -1.0, 1.0, ALU.mult, ALU.add), sv)
        p.op("vector", lambda E: E.tensor_scalar(negu[:, :], negu[:, :], -1.0, None, ALU.mult), sv)
        p.wait("vector", sv)

        for c in range(16):
            p.wait("sync", sv, p.val(sv))
            p.dma("sync", xb[:, :], xn_h[c * 128:(c + 1) * 128, :], sd)
            p.dma("sync", zb[:, :], zp_h[c * 128:(c + 1) * 128, :], sd)
            p.wait("vector", sd)
            p.wait("vector", sg)
            p.op("vector", lambda E, c=c: E.tensor_scalar(
                sq[:, :], zb[:, :], gt[:, c:c + 1], None, ALU.mult), sv)
            p.op("vector", lambda E, c=c: E.tensor_scalar(
                t1[:, :], xb[:, :], gmt[:, c:c + 1], None, ALU.mult), sv)
            p.op("vector", lambda E, c=c: E.tensor_tensor(
                xst[:, c * T:(c + 1) * T], sq[:, :], t1[:, :], ALU.add), sv)
        p.wait("tensor", sv)
        for c in range(16):
            for tt in range(4):
                p.op("tensor", lambda E, c=c, tt=tt: E.matmul(
                    ps[0:1, tt * 512:(tt + 1) * 512], onescb[:, :],
                    xst[:, c * T + tt * 512:c * T + (tt + 1) * 512],
                    start=(c == 0), stop=(c == 15)),
                    sm if (c == 15 and tt == 3) else None)
        p.wait("vector", sm)
        MU, RS, NB, V1 = 0, 1, 2, 3
        p.op("vector", lambda E: E.tensor_scalar(
            srow[0:1, MU * T:(MU + 1) * T], ps[0:1, :], 1.0 / D, None, ALU.mult), sv)
        p.wait("scalar", sv)
        for c in range(16):
            p.wait("scalar", sm, p.val(sm))
            p.op("scalar", lambda E, c=c: E.activation(
                sq[:, :], xst[:, c * T:(c + 1) * T], AF.Square), ss)
            p.wait("tensor", ss)
            for tt in range(4):
                p.op("tensor", lambda E, c=c, tt=tt: E.matmul(
                    ps[0:1, tt * 512:(tt + 1) * 512], onesc[:, :],
                    sq[:, tt * 512:(tt + 1) * 512],
                    start=(c == 0), stop=(c == 15)),
                    sm if (tt == 3) else None)
        p.wait("vector", sm)
        p.op("vector", lambda E: E.tensor_scalar(
            srow[0:1, V1 * T:(V1 + 1) * T], ps[0:1, :], 1.0 / D, None, ALU.mult), sv)
        p.wait("vector", sv)
        p.op("vector", lambda E: E.tensor_tensor(
            srow[0:1, RS * T:(RS + 1) * T], srow[0:1, MU * T:(MU + 1) * T],
            srow[0:1, MU * T:(MU + 1) * T], ALU.mult), sv)
        p.wait("vector", sv)
        p.op("vector", lambda E: E.tensor_tensor(
            srow[0:1, V1 * T:(V1 + 1) * T], srow[0:1, V1 * T:(V1 + 1) * T],
            srow[0:1, RS * T:(RS + 1) * T], ALU.subtract), sv)
        p.wait("scalar", sv)
        p.op("scalar", lambda E: E.activation(
            srow[0:1, RS * T:(RS + 1) * T], srow[0:1, V1 * T:(V1 + 1) * T],
            AF.Ln, bias=ct[0:1, C_EPS:C_EPS + 1]), ss)
        p.wait("scalar", ss)
        p.op("scalar", lambda E: E.activation(
            srow[0:1, RS * T:(RS + 1) * T], srow[0:1, RS * T:(RS + 1) * T],
            AF.Exp, scale=-0.5), ss)
        p.wait("vector", ss)
        p.op("vector", lambda E: E.tensor_tensor(
            srow[0:1, NB * T:(NB + 1) * T], srow[0:1, MU * T:(MU + 1) * T],
            srow[0:1, RS * T:(RS + 1) * T], ALU.mult), sv)
        p.wait("vector", sv)
        p.op("vector", lambda E: E.tensor_scalar(
            srow[0:1, NB * T:(NB + 1) * T], srow[0:1, NB * T:(NB + 1) * T],
            -1.0, None, ALU.mult), sv)
        p.wait("vector", sv)
        p.wait("tensor", sv)
        for half, col in ((0, RS), (1, NB)):
            for tt in range(4):
                p.op("tensor", lambda E, col=col, tt=tt: E.matmul(
                    psb[:, :], onesr[:, :],
                    srow[0:1, col * T + tt * 512:col * T + (tt + 1) * 512],
                    start=True, stop=True), sm)
                p.wait("vector", sm)
                p.op("vector", lambda E, half=half, tt=tt: E.tensor_copy(
                    bc[:, half * T + tt * 512:half * T + (tt + 1) * 512],
                    psb[:, :]), sv)
                p.wait("tensor", sv)
        for tt in range(4):
            p.wait("vector", sm)
            for dc in range(16):
                p.op("vector", lambda E, tt=tt, dc=dc: E.tensor_tensor(
                    sq[:, 0:512], xst[:, dc * T + tt * 512:dc * T + (tt + 1) * 512],
                    bc[:, tt * 512:(tt + 1) * 512], ALU.mult), sv)
                p.op("vector", lambda E, tt=tt, dc=dc: E.tensor_tensor(
                    sq[:, 0:512], sq[:, 0:512],
                    bc[:, T + tt * 512:T + (tt + 1) * 512], ALU.add), sv)
                p.op("vector", lambda E, dc=dc: E.tensor_scalar(
                    xnrm[:, dc * 512:(dc + 1) * 512], sq[:, 0:512],
                    lng[:, dc:dc + 1], lnb[:, dc:dc + 1], ALU.mult, ALU.add), sv)
            p.wait("tensor", sv)
            p.wait("tensor", sg)
            p.wait("tensor", ss)
            for dc in range(16):
                p.op("tensor", lambda E, dc=dc: E.matmul(
                    pskv[:, :], wkv_w[:, dc * 128:(dc + 1) * 128],
                    xnrm[:, dc * 512:(dc + 1) * 512],
                    start=(dc == 0), stop=(dc == 15)),
                    sm if dc == 15 else None)
            p.wait("scalar", sm)
            p.op("scalar", lambda E: E.activation(kwk[0:64, 0:512], pskv[0:64, :], AF.Sigmoid), ss)
            p.wait("vector", ss)
            p.op("vector", lambda E: E.tensor_scalar(
                kwk[0:64, 512:1024], kwk[0:64, 0:512], negu[:, 0:1], None, ALU.mult), sv)
            p.wait("scalar", sv)
            p.op("scalar", lambda E: E.activation(kwk[0:64, 1024:1536], kwk[0:64, 512:1024], AF.Exp), ss)
            p.wait("vector", ss)
            p.op("vector", lambda E, tt=tt: E.tensor_tensor(
                wkvt[0:64, tt * 512:(tt + 1) * 512], kwk[0:64, 1024:1536],
                pskv[64:128, :], ALU.mult), sv, 1)
        p.wait("vector", sv)
        p.op("vector", lambda E: E.tensor_tensor_scan(
            stt[0:64, :], wcol[0:64, 0:1].broadcast_to([64, T]), wkvt[0:64, :],
            ct[0:64, C_ZERO:C_ZERO + 1], ALU.mult, ALU.add), sv)
        p.wait("sync", sv)
        p.dma("sync", st_h, stt[0:64, :], so)
        p.dma("sync", sf_h, stt[0:64, T - 1:T], so)
        p.wait("sync", so)
        p.wait("sync", sd)
        p.wait("sync", sg)
        if fctx is None:
            block = es.enter_context(nc.Block())
            p.run(block)
        else:
            _barrier(p, S)
    return nc



def build_l1proj(fctx=None, pre=""):
    nc = fctx.nc if fctx else _mk_nc()
    st_in = _io(fctx, nc, pre + "states", [64, T], F32, "ExternalInput")
    sp_h = _io(fctx, nc, pre + "sp", [64, 1], F32, "ExternalInput")
    sz_h = _io(fctx, nc, pre + "szero", [64, 1], F32, "ExternalInput")
    tdr_h = _io(fctx, nc, pre + "tdrow", [1, 64], F32, "ExternalInput")
    pos_h = _io(fctx, nc, pre + "posr", [1, T], F32, "ExternalInput")
    wop_h = _io(fctx, nc, pre + "woutpT", [64, D], F32, "ExternalInput")
    at_h = _io(fctx, nc, pre + "attn1", [D, T], F32, "ExternalOutput")
    es = ExitStack()
    with es:
        stt = es.enter_context(nc.sbuf_tensor(pre + "s_stt", [64, T], F32))
        stb = es.enter_context(nc.sbuf_tensor(pre + "stb", [64, T], BF16))
        wp = es.enter_context(nc.sbuf_tensor(pre + "wp", [64, T], F32))
        spc = es.enter_context(nc.sbuf_tensor(pre + "spc", [64, 1], F32))
        szc = es.enter_context(nc.sbuf_tensor(pre + "szc", [64, 1], F32))
        tdr = es.enter_context(nc.sbuf_tensor(pre + "tdr", [1, 64], F32))
        posr = es.enter_context(nc.sbuf_tensor(pre + "s_posr", [1, T], F32))
        wob = es.enter_context(nc.sbuf_tensor(pre + "wob", [64, D], BF16))
        ob = es.enter_context(nc.sbuf_tensor(pre + "ob", [128, 2 * T], F32))
        psw = es.enter_context(nc.psum_tensor(pre + "psw", [64, T], F32))
        pso = es.enter_context(nc.psum_tensor(pre + "pso", [128, T], F32))
        S = fctx.S if fctx else _sems(es, nc, ALL_SEMS)
        (sd, sg, so, so1, sv, ss, sm, se) = (
            S[k] for k in ["sd", "sg", "so", "so1", "sv", "ss", "sm", "se"])
        soP = [so, so1]
        if fctx is None:
            p = Prog(nc)
            p.autosem = {"vector": sv, "scalar": ss}
        else:
            p = fctx.p
        p.dma("sync", stt[0:64, :], st_in, sd)
        p.dma("sync", spc[0:64, :], sp_h, sd)
        p.dma("sync", szc[0:64, :], sz_h, sd)
        p.dma("sync", tdr[0:1, :], tdr_h, sd)
        p.dma("sync", posr[0:1, :], pos_h, sd)
        p.dma("gpsimd", wob[0:64, :], wop_h, sg)
        p.wait("tensor", sd)
        p.wait("tensor", sg)
        for tt in range(4):
            p.op("tensor", lambda E, tt=tt: E.matmul(
                psw[0:64, tt * 512:(tt + 1) * 512], tdr[0:1, :],
                posr[0:1, tt * 512:(tt + 1) * 512], start=True, stop=True),
                sm if tt == 3 else None)
        p.wait("scalar", sm)
        p.op("scalar", lambda E: E.activation(wp[0:64, :], psw[0:64, :], AF.Exp), ss)
        p.wait("vector", ss)
        p.op("vector", lambda E: E.tensor_tensor(spc[0:64, :], spc[0:64, :], szc[0:64, :], ALU.mult), sv)
        p.wait("vector", sv)
        p.op("vector", lambda E: E.tensor_scalar(wp[0:64, :], wp[0:64, :], spc[0:64, 0:1], None, ALU.mult), sv)
        p.op("vector", lambda E: E.tensor_tensor(stt[0:64, :], stt[0:64, :], wp[0:64, :], ALU.add), sv)
        p.wait("vector", sv)
        p.op("vector", lambda E: E.tensor_copy(stb[0:64, :], stt[0:64, :]), sv)
        p.wait("tensor", sv)
        for dc in range(16):
            for tt in range(4):
                p.op("tensor", lambda E, dc=dc, tt=tt: E.matmul(
                    pso[:, tt * 512:(tt + 1) * 512],
                    wob[0:64, dc * 128:(dc + 1) * 128],
                    stb[0:64, tt * 512:(tt + 1) * 512],
                    start=True, stop=True),
                    sm if tt == 3 else None)
            p.wait("vector", sm)
            db = dc % 2
            if dc >= 2:
                p.wait("vector", soP[db])
            p.op("vector", lambda E, db=db: E.tensor_copy(ob[:, db * T:(db + 1) * T], pso[:, :]), sv)
            p.wait("tensor", sv)
            p.wait("sync", sv)
            p.dma("sync", at_h[dc * 128:(dc + 1) * 128, :], ob[:, db * T:(db + 1) * T], soP[db])
        for s_ in (so, so1, sd, sg):
            p.wait("sync", s_)
        p.wait("sync", sg)
        if fctx is None:
            block = es.enter_context(nc.Block())
            p.run(block)
        else:
            _barrier(p, S)
    return nc



def build_ffn1final(debug=False, fctx=None, pre=""):
    nc = fctx.nc if fctx else _mk_nc()
    xn_h = _io(fctx, nc, pre + "xn2", [D, T], BF16, "ExternalInput")
    w1_h = _io(fctx, nc, pre + "w1T", [D, H1], F32, "ExternalInput")
    w2_h = _io(fctx, nc, pre + "w2T", [D, H1], F32, "ExternalInput")
    w3_h = _io(fctx, nc, pre + "w3T", [H1, D], F32, "ExternalInput")
    x1_h = _io(fctx, nc, pre + "x1", [D, T], F32, "ExternalInput")
    at_h = _io(fctx, nc, pre + "attn1", [D, T], F32, "ExternalInput")
    consts = _io(fctx, nc, pre + "consts", [128, 8], F32, "ExternalInput")
    onesc_h = _io(fctx, nc, pre + "onesc", [128, 1], F32, "ExternalInput")
    onesr_h = _io(fctx, nc, pre + "onesr", [1, 128], F32, "ExternalInput")
    y_h = _io(fctx, nc, pre + "y", [D, T], F32, "ExternalOutput")
    kind = "ExternalOutput" if debug else "Internal"
    hid_h = nc.dram_tensor("hidden", [H1, T], BF16, kind=kind)
    w3q_h = nc.dram_tensor("w3q", [H1, D], BF16, kind=kind)
    gdump_h = (nc.dram_tensor("gdump", [128, 4], F32, kind="ExternalOutput").ap()
               if debug else None)
    accdump_h = (nc.dram_tensor("accdump", [128, 4], F32, kind="ExternalOutput").ap()
                 if debug else None)
    sgmdump_h = (nc.dram_tensor("sgmdump", [128, 512], F32, kind="ExternalOutput").ap()
                 if debug else None)
    assert fctx is None or not debug
    es = ExitStack()
    with es:
        xn = es.enter_context(nc.sbuf_tensor(pre + "s_xn", [128, 16 * T], BF16))
        ct = es.enter_context(nc.sbuf_tensor(pre + "ct", [128, 8], F32))
        onesc = es.enter_context(nc.sbuf_tensor(pre + "s_onesc", [128, 1], F32))
        onesr = es.enter_context(nc.sbuf_tensor(pre + "s_onesr", [1, 128], F32))
        arena = es.enter_context(nc.sbuf_tensor(pre + "arena", [128, 16384], I32))
        acc = es.enter_context(nc.sbuf_tensor(pre + "acc", [128, 4], F32))
        ginv = es.enter_context(nc.sbuf_tensor(pre + "ginv", [128, 4], F32))
        grow = es.enter_context(nc.sbuf_tensor(pre + "grow", [1, 4], F32))
        w1q = es.enter_context(nc.sbuf_tensor(pre + "w1q", [128, 16 * 512], BF16))
        w2q = es.enter_context(nc.sbuf_tensor(pre + "w2q", [128, 16 * 512], BF16))
        sgm = es.enter_context(nc.sbuf_tensor(pre + "sgm", [128, 512], F32))
        hb = es.enter_context(nc.sbuf_tensor(pre + "hb", [128, 2 * T], BF16))
        w3b = es.enter_context(nc.sbuf_tensor(pre + "w3b", [128, 44 * 128], BF16))
        x1b = es.enter_context(nc.sbuf_tensor(pre + "x1b", [128, 512], F32))
        atb = es.enter_context(nc.sbuf_tensor(pre + "atb", [128, 512], F32))
        yb = es.enter_context(nc.sbuf_tensor(pre + "yb", [128, 2 * 512], F32))
        t1 = es.enter_context(nc.sbuf_tensor(pre + "t1", [128, 512], F32))
        ps1 = es.enter_context(nc.psum_tensor(pre + "ps1", [128, T], F32))
        ps2 = es.enter_context(nc.psum_tensor(pre + "ps2", [128, T], F32))
        S = fctx.S if fctx else _sems(es, nc, ALL_SEMS)
        (sd, sg, so, so1, sdp, sdh, sv, ss, sm, se) = (
            S[k] for k in ["sd", "sg", "so", "so1", "sdp", "sdh", "sv", "ss", "sm", "se"])
        soP = [so, so1]
        if fctx is None:
            p = Prog(nc)
            p.autosem = {"vector": sv, "scalar": ss}
        else:
            p = fctx.p
        wf = arena.ap().bitcast(F32)[:, 0:8192]
        qi = arena.ap()[:, 8192:16384]
        hidt = arena.ap().bitcast(BF16)[:, 0:22528]

        p.dma("sync", ct[:, :], consts, sd)
        p.dma("sync", onesc[:, :], onesc_h, sd)
        p.dma("sync", onesr[:, :], onesr_h, sd)
        for c in range(16):
            p.dma("sync", xn[:, c * T:(c + 1) * T], xn_h[c * 128:(c + 1) * 128, :], sd)

        for wi, (wh, nch, wid) in enumerate(((w1_h, 16, H1), (w2_h, 16, H1), (w3_h, 44, D))):
            for c in range(nch):
                p.wait("sync", sv, p.val(sv))
                p.dma("sync", wf[:, 0:wid], wh[c * 128:(c + 1) * 128, :], sd)
                p.wait("vector", sd)
                p.op("vector", lambda E, wid=wid, c=c: E.reduce_sum(
                    sgm[:, c:c + 1], wf[:, 0:wid], mybir.AxisListType.X,
                    apply_absolute_value=True), sv)
            p.wait("vector", sv)
            p.op("vector", lambda E, wi=wi, nch=nch: E.reduce_sum(
                acc[:, wi:wi + 1], sgm[:, 0:nch], mybir.AxisListType.X), sv)
            p.wait("vector", sv)
        p.wait("tensor", sv)
        p.op("tensor", lambda E: E.matmul(ps1[0:1, 0:3], onesc[:, :], acc[:, 0:3],
                                          start=True, stop=True), sm)
        p.wait("vector", sm)
        p.op("vector", lambda E: E.tensor_scalar(
            grow[0:1, 0:2], ps1[0:1, 0:2], 1.0 / (D * H1), 1e-8, ALU.mult, ALU.add), sv)
        p.op("vector", lambda E: E.tensor_scalar(
            grow[0:1, 2:3], ps1[0:1, 2:3], 1.0 / (D * H1), 1e-8, ALU.mult, ALU.add), sv)
        p.wait("vector", sv)
        p.op("vector", lambda E: E.reciprocal(grow[0:1, 0:3], grow[0:1, 0:3]), sv)
        p.wait("vector", sv)
        p.wait("tensor", sv)
        p.op("tensor", lambda E: E.matmul(ps2[:, 0:3], onesr[0:1, :], grow[0:1, 0:3],
                                          start=True, stop=True), sm)
        p.wait("vector", sm)
        p.op("vector", lambda E: E.tensor_copy(ginv[:, 0:3], ps2[:, 0:3]), sv)
        p.marks["f1_psum_free"] = p.val(sv)
        if debug:
            p.wait("sync", sv)
            p.dma("sync", gdump_h, ginv[:, :], so)
            p.dma("sync", accdump_h, acc[:, :], so)
            p.dma("sync", sgmdump_h, sgm[:, :], so)

        SO_G = p.val(S["so"])
        def quant(dst, src, gcol, n):
            p.op("vector", lambda E: E.tensor_scalar(
                qi[:, 0:n], src, ginv[:, gcol:gcol + 1], None, ALU.mult), sv)
            p.op("vector", lambda E: E.tensor_scalar(
                dst, qi[:, 0:n], -1.0, 1.0, ALU.max, ALU.min), sv)

        for c in range(44):
            p.wait("sync", sv, p.val(sv))
            p.dma("sync", wf[:, 0:D], w3_h[c * 128:(c + 1) * 128, :], sd)
            p.wait("vector", sd)
            if c >= 1:
                p.wait("vector", so)
            quant(hb[:, 0:D], wf[:, 0:D], 2, D)
            p.wait("sync", sv)
            p.dma("sync", w3q_h[c * 128:(c + 1) * 128, :], hb[:, 0:D], so)

        p.wait("vector", so)
        for hs in range(11):
            h0 = hs * 512
            for wi, (wh, wq) in enumerate(((w1_h, w1q), (w2_h, w2q))):
                p.wait("sync", sv, p.val(sv))
                p.dma("sync", wf[:, 0:16 * 512],
                      wh.rearrange("(dc k) h -> k dc h", k=128)[:, :, h0:h0 + 512], sd)
                p.wait("vector", sd)
                p.wait("vector", sg)
                p.wait("vector", sm)
                quant(wq[:, :], wf[:, 0:16 * 512], wi, 16 * 512)
            p.wait("tensor", sv)
            for hsub in range(4):
                hc0 = hsub * 128
                p.wait("tensor", sv, p.marks.get("f1_psum_free", 0))
                for ps_, wq in ((ps1, w1q), (ps2, w2q)):
                    for tt in range(4):
                        for dc in range(16):
                            p.op("tensor", lambda E, ps_=ps_, wq=wq, tt=tt, dc=dc, hc0=hc0: E.matmul(
                                ps_[:, tt * 512:(tt + 1) * 512],
                                wq[:, dc * 512 + hc0:dc * 512 + hc0 + 128],
                                xn[:, dc * T + tt * 512:dc * T + (tt + 1) * 512],
                                start=(dc == 0), stop=(dc == 15)),
                                sm if (dc == 15) else None)
                i = hs * 4 + hsub
                ib = i % 2
                p.wait("scalar", sm)
                if i >= 2:
                    p.wait("vector", soP[ib])
                for tt in range(4):
                    p.wait("scalar", sv, p.val(sv))
                    p.op("scalar", lambda E, tt=tt: E.activation(
                        sgm[:, :], ps1[:, tt * 512:(tt + 1) * 512], AF.Sigmoid), ss)
                    p.wait("vector", ss)
                    p.op("vector", lambda E, ib=ib, tt=tt: E.tensor_tensor(
                        hb[:, ib * T + tt * 512:ib * T + (tt + 1) * 512],
                        sgm[:, :], ps2[:, tt * 512:(tt + 1) * 512], ALU.mult), sv)
                p.marks["f1_psum_free"] = p.val(sv)
                p.wait("sync", sv)
                p.dma("sync", hid_h[h0 + hc0:h0 + hc0 + 128, :],
                      hb[:, ib * T:(ib + 1) * T], soP[ib])

        for s_ in (so, so1):
            p.wait("sync", s_)
            p.wait("gpsimd", s_)
        for tt in range(4):
            p.wait("sync", sm)
            p.dma("sync", hidt[:, :],
                  hid_h.rearrange("(hcc k) t -> k hcc t", k=128)[:, :, tt * 512:(tt + 1) * 512], sdh)
            for dc in range(16):
                i = tt * 16 + dc
                p.wait("gpsimd", sm)
                p.dma("gpsimd", w3b[:, :],
                      w3q_h.rearrange("(hcc k) d -> k hcc d", k=128)[:, :, dc * 128:(dc + 1) * 128], sg)
                p.wait("sync", sv, p.val(sv))
                p.dma("sync", x1b[:, :], x1_h[dc * 128:(dc + 1) * 128, tt * 512:(tt + 1) * 512], sdp)
                p.dma("sync", atb[:, :], at_h[dc * 128:(dc + 1) * 128, tt * 512:(tt + 1) * 512], sdp)
                p.wait("tensor", sdh)
                p.wait("tensor", sg)
                p.wait("tensor", ss, p.marks.get("f1d_psum_free", 0))
                for hcc in range(44):
                    p.op("tensor", lambda E, hcc=hcc: E.matmul(
                        ps1[:, 0:512], w3b[:, hcc * 128:(hcc + 1) * 128],
                        hidt[:, hcc * 512:(hcc + 1) * 512],
                        start=(hcc == 0), stop=(hcc == 43)),
                        sm if hcc == 43 else None)
                p.wait("scalar", sm)
                p.wait("scalar", sv, p.val(sv))
                p.op("scalar", lambda E: E.activation(t1[:, :], ps1[:, 0:512], AF.Copy), ss)
                p.marks["f1d_psum_free"] = p.val(ss)
                p.wait("vector", ss)
                p.wait("vector", sdp)
                b = i % 2
                if i >= 2:
                    p.wait("vector", soP[b])
                p.op("vector", lambda E, b=b: E.tensor_tensor(
                    yb[:, b * 512:(b + 1) * 512], t1[:, :], x1b[:, :], ALU.add), sv)
                p.op("vector", lambda E, b=b: E.tensor_tensor(
                    yb[:, b * 512:(b + 1) * 512], yb[:, b * 512:(b + 1) * 512],
                    atb[:, :], ALU.add), sv)
                p.wait("sync", sv)
                p.dma("sync", y_h[dc * 128:(dc + 1) * 128, tt * 512:(tt + 1) * 512],
                      yb[:, b * 512:(b + 1) * 512], soP[b])
        for s_ in (so, so1, sd, sg, sdp, sdh):
            p.wait("sync", s_)
        if fctx is None:
            block = es.enter_context(nc.Block())
            p.run(block)
        else:
            _barrier(p, S)
    return nc





def emit_exchange(p, S, nc, tiles, idx, src_dram, dst_dram, sh_data, sh_flag,
                  rows, cols, magic):
    sx = S["sx"]
    spl = S[f"spl{idx}"]
    ht, mgt, flt = tiles["ht"], tiles["mgt"], tiles["flt"]
    elems = rows * cols
    t1 = p._bump(sx, 16)
    t2 = p._bump(sx, 16)
    t3 = p._bump(sx, 16)
    pat = [[cols, rows], [1, cols]]

    def go(E, src=src_dram, dst=dst_dram, elems=elems, idx=idx, magic=magic,
           t1=t1, t2=t2, t3=t3, pat=pat):
        with E.register(f"h{idx}") as h, E.register(f"off{idx}") as off, \
             E.register(f"poff{idx}") as poff, E.register(f"ptgt{idx}") as ptgt, \
             E.register(f"fv{idx}") as fv, E.register(f"cnd{idx}") as cnd, \
             E.register(f"foff{idx}") as foff:
            E.reg_load(h, ht[:1, :1])
            E.reg_mul(off, h, elems)
            E.dma_start(out=bass.AP(sh_data, off, pat), in_=src).then_inc(sx, 16)
            E.wait_ge(sx, t1)
            E.reg_mul(foff, h, 4)
            E.reg_add(foff, foff, idx)
            E.dma_start(out=bass.AP(sh_flag, foff, [[1, 1], [1, 1]]),
                        in_=mgt[0:1, idx:idx + 1]).then_inc(sx, 16)
            E.wait_ge(sx, t2)
            E.reg_sub(poff, 1, h)
            E.reg_mul(foff, poff, 4)
            E.reg_add(foff, foff, idx)
            E.reg_mov(fv, 0)
            E.reg_mov(ptgt, 16)
            E.reg_mov(cnd, 1)
            pflag = bass.AP(sh_flag, foff, [[1, 1], [1, 1]])
            with E.While(cnd):
                E.dma_start(out=flt[:1, :1], in_=pflag).then_inc(spl, 16)
                E.wait_ge(spl, ptgt)
                E.reg_load(fv, flt[:1, :1])
                E.reg_add(ptgt, ptgt, 16)
                E.reg_alu(cnd, fv, magic, mybir.AluOpType.not_equal)
            E.reg_mul(off, poff, elems)
            return E.dma_start(out=dst, in_=bass.AP(sh_data, off, pat))
    p.on("gpsimd", lambda E: go(E).then_inc(S["sx"], 16))


def build_fused():
    nc = _mk_nc()
    S_es = ExitStack()
    S = {n: S_es.enter_context(nc.semaphore(n)) for n in ALL_SEMS}
    p = Prog(nc)
    p.autosem = {"vector": S["sv"], "scalar": S["ss"]}
    fctx = FCtx(nc, p, S)
    io = fctx.io

    def ext(name, shape, dt):
        io_ap = nc.dram_tensor(name, shape, dt, kind="ExternalInput").ap()
        return io_ap

    def internal(name, shape, dt):
        return nc.dram_tensor(name, shape, dt, kind="Internal").ap()

    x0 = ext("x0", [D, T], F32)
    consts = ext("consts", [128, 8], F32)
    onesc = ext("onesc", [128, 1], F32)
    onesr = ext("onesr", [1, 128], F32)
    onescb = ext("onescb", [128, 1], BF16)
    winT = ext("winT", [D, 3 * D], F32)
    binc = ext("binc", [128, 48], F32)
    cosr = ext("cosr", [512, T], F32)
    sinr = ext("sinr", [512, T], F32)
    woutT = ext("woutT", [D, D], F32)
    boutc = ext("boutc", [128, 16], F32)
    gpk = ext("gpk", [1024, NF], I32)
    gsc = ext("gsc", [64, NF], F32)
    upk = ext("upk", [1024, NF], I32)
    usc = ext("usc", [64, NF], F32)
    dpk = ext("dpk", [NF // 2, D], I32)
    dsc = ext("dsc", [512, D], F32)
    wshT = ext("wshT", [D, D], F32)
    gatec = ext("gatec", [128, 16], F32)
    lngc = ext("lngc", [128, 16], F32)
    lnbc = ext("lnbc", [128, 16], F32)
    wkvT = ext("wkvT", [D, 128], F32)
    tdc = ext("tdc", [64, 1], F32)
    tfc = ext("tfc", [64, 1], F32)
    tdrow = ext("tdrow", [1, 64], F32)
    posr = ext("posr", [1, T], F32)
    w1T = ext("w1T", [D, H1], F32)
    w2T = ext("w2T", [D, H1], F32)
    w3T = ext("w3T", [H1, D], F32)
    woutpT = ext("woutpT", [64, D], F32)
    szero = ext("szero", [64, 1], F32)
    halfI = ext("halfI", [1, 1], I32)
    magics = ext("magics", [1, 3], I32)
    y = nc.dram_tensor("y", [D, T], F32, kind="ExternalOutput").ap()

    xn = internal("i_xn", [D, T], BF16)
    apart = internal("i_apart", [128, 16], F32)
    apart_p = internal("i_apart_p", [128, 16], F32)
    px1 = internal("i_px1", [D, T], F32)
    ddec = internal("i_ddec", [NF, D], BF16)
    inter = internal("i_inter", [NF, T], BF16)
    x1 = internal("i_x1", [D, T], F32)
    xn2 = internal("i_xn2", [D, T], BF16)
    z = internal("i_z", [D, T], BF16)
    zp = internal("i_zp", [D, T], BF16)
    states = internal("i_states", [64, T], F32)
    sfin = internal("i_sfin", [64, 1], F32)
    sfin_p = internal("i_sfin_p", [64, 1], F32)
    attn1 = internal("i_attn1", [D, T], F32)
    sh_ap = nc.dram_tensor("sh_ap", [2, 128 * 16], F32, addr_space="Shared")
    sh_z = nc.dram_tensor("sh_z", [2, D * T], BF16, addr_space="Shared")
    sh_sf = nc.dram_tensor("sh_sf", [2, 64], F32, addr_space="Shared")
    sh_flag = nc.dram_tensor("sh_flag", [2, 4], I32, addr_space="Shared")

    ex_es = ExitStack()
    ht = ex_es.enter_context(nc.sbuf_tensor("x_ht", [1, 1], I32))
    mgt = ex_es.enter_context(nc.sbuf_tensor("x_mgt", [1, 3], I32))
    flt = ex_es.enter_context(nc.sbuf_tensor("x_flt", [1, 1], I32))
    extiles = dict(ht=ht, mgt=mgt, flt=flt)
    p.dma("gpsimd", ht[:, :], halfI, S["sg"])
    p.dma("gpsimd", mgt[:, :], magics, S["sg"])
    p.wait("gpsimd", S["sg"])

    io.update({"ln0_xp0": x0, "ln0_consts": consts, "ln0_onesc": onesc,
               "ln0_onesr": onesr, "ln0_xn": xn})
    build_ln(1, fctx=fctx, pre="ln0_")
    io.update({"kv_xn": xn, "kv_winT": winT, "kv_binc": binc, "kv_cosr": cosr,
               "kv_sinr": sinr, "kv_consts": consts, "kv_onesc": onesc,
               "kv_onesr": onesr, "kv_apart": apart})
    build_kva(fctx=fctx, pre="kv_")
    emit_exchange(p, S, nc, extiles, 0, apart, apart_p, sh_ap, sh_flag,
                  128, 16, MAGIC0)
    _barrier(p, S)
    io.update({"qa_xn": xn, "qa_x0": x0, "qa_winT": winT, "qa_binc": binc,
               "qa_cosr": cosr, "qa_sinr": sinr, "qa_woutT": woutT,
               "qa_boutc": boutc, "qa_amine": apart, "qa_apartner": apart_p,
               "qa_px1": px1})
    build_qattn(fctx=fctx, pre="qa_")
    io.update({"f0a_xn": xn, "f0a_gpk": gpk, "f0a_gsc": gsc, "f0a_upk": upk,
               "f0a_usc": usc, "f0a_dpk": dpk, "f0a_dsc": dsc,
               "f0a_consts": consts, "f0a_ddec": ddec, "f0a_inter": inter})
    build_ffn0a(fctx=fctx, pre="f0a_")
    io.update({"f0b_inter": inter, "f0b_ddec": ddec, "f0b_px1": px1,
               "f0b_x1": x1})
    build_ffn0b(fctx=fctx, pre="f0b_")
    io.update({"ln1_xp0": x1, "ln1_consts": consts, "ln1_onesc": onesc,
               "ln1_onesr": onesr, "ln1_xn": xn2})
    build_ln(1, fctx=fctx, pre="ln1_")
    io.update({"z_xn2": xn2, "z_wshT": wshT, "z_z": z})
    build_zmm(fctx=fctx, pre="z_")
    emit_exchange(p, S, nc, extiles, 1, z, zp, sh_z, sh_flag, D, T, MAGIC1)
    _barrier(p, S)
    io.update({"l1_xn2": xn2, "l1_zp": zp, "l1_gatec": gatec, "l1_lngc": lngc,
               "l1_lnbc": lnbc, "l1_wkvT": wkvT, "l1_tdc": tdc, "l1_tfc": tfc,
               "l1_consts": consts, "l1_onesc": onesc, "l1_onescb": onescb,
               "l1_onesr": onesr, "l1_states": states, "l1_sfin": sfin})
    build_l1attn(fctx=fctx, pre="l1_")
    emit_exchange(p, S, nc, extiles, 2, sfin, sfin_p, sh_sf, sh_flag, 64, 1, MAGIC2)
    _barrier(p, S)
    io.update({"pj_states": states, "pj_sp": sfin_p, "pj_szero": szero,
               "pj_tdrow": tdrow, "pj_posr": posr, "pj_woutpT": woutpT,
               "pj_attn1": attn1})
    build_l1proj(fctx=fctx, pre="pj_")
    io.update({"f1_xn2": xn2, "f1_w1T": w1T, "f1_w2T": w2T, "f1_w3T": w3T,
               "f1_x1": x1, "f1_attn1": attn1, "f1_consts": consts,
               "f1_onesc": onesc, "f1_onesr": onesr, "f1_y": y})
    build_ffn1final(fctx=fctx, pre="f1_")

    block = S_es.enter_context(nc.Block())
    p.run(block)
    S_es.close()
    ex_es.close()
    return nc



_P = np.concatenate([np.arange(0, D, 2), np.arange(1, D, 2)])
_PINV = np.argsort(_P)
_R = np.concatenate([np.arange(0, 1024, 2), np.arange(1, 1024, 2), np.arange(1024, D)])
_NP = np.concatenate([np.arange(0, NF, 2), np.arange(1, NF, 2)])

_NC_CACHE = {}


def _nc(key, builder, *a, **kw):
    if key not in _NC_CACHE:
        _NC_CACHE[key] = builder(*a, **kw)
    return _NC_CACHE[key]


def _cc(x):
    return np.ascontiguousarray(x)


def kernel(L0_attn_X, L0_attn_W_in, L0_attn_b_in, L0_attn_W_out, L0_attn_b_out,
           L0_ffn_gate_W_packed, L0_ffn_gate_scales, L0_ffn_up_W_packed,
           L0_ffn_up_scales, L0_ffn_down_W_packed, L0_ffn_down_scales,
           L1_attn_W_key, L1_attn_W_value, L1_attn_W_output, L1_attn_W_shift,
           L1_attn_gate, L1_attn_time_decay, L1_attn_time_first,
           L1_attn_ln_gamma, L1_attn_ln_beta, L1_ffn_w1, L1_ffn_w2, L1_ffn_w3):
    f32 = np.float32
    X = np.asarray(L0_attn_X, f32)
    cores = list(range(8))

    rows_qkv = np.concatenate([_R, D + _R, 2 * D + _R])
    winT = _cc(np.asarray(L0_attn_W_in, f32)[np.ix_(rows_qkv, _P)].T)
    binc = _cc(np.asarray(L0_attn_b_in, f32)[rows_qkv].reshape(48, 128).T)
    woutT = _cc(np.asarray(L0_attn_W_out, f32)[np.ix_(_P, _R)].T)
    boutc = _cc(np.asarray(L0_attn_b_out, f32)[_P].reshape(16, 128).T)
    gpk = _cc(np.asarray(L0_ffn_gate_W_packed)[_NP].T.astype(np.int32))
    gsc = _cc(np.asarray(L0_ffn_gate_scales, f32)[_NP].T)
    upk = _cc(np.asarray(L0_ffn_up_W_packed)[_NP].T.astype(np.int32))
    usc = _cc(np.asarray(L0_ffn_up_scales, f32)[_NP].T)
    dpk = _cc(np.asarray(L0_ffn_down_W_packed)[_P].T.astype(np.int32))
    dsc = _cc(np.asarray(L0_ffn_down_scales, f32)[_P].T)
    wshT = _cc(np.asarray(L1_attn_W_shift, f32)[np.ix_(_P, _P)].T)
    gatec = _cc(np.asarray(L1_attn_gate, f32)[_P].reshape(16, 128).T)
    lngc = _cc(np.asarray(L1_attn_ln_gamma, f32)[_P].reshape(16, 128).T)
    lnbc = _cc(np.asarray(L1_attn_ln_beta, f32)[_P].reshape(16, 128).T)
    wkvT = _cc(np.concatenate([np.asarray(L1_attn_W_key, f32)[:, _P].T,
                               np.asarray(L1_attn_W_value, f32)[:, _P].T], axis=1))
    woutpT = _cc(np.asarray(L1_attn_W_output, f32)[_P].T)
    tdc = _cc(np.asarray(L1_attn_time_decay, f32).reshape(64, 1))
    tfc = _cc(np.asarray(L1_attn_time_first, f32).reshape(64, 1))
    tdrow = _cc(np.asarray(L1_attn_time_decay, f32).reshape(1, 64))
    w1T = _cc(np.asarray(L1_ffn_w1, f32)[:, _P].T)
    w2T = _cc(np.asarray(L1_ffn_w2, f32)[:, _P].T)
    w3T = _cc(np.asarray(L1_ffn_w3, f32)[_P].T)

    consts = np.zeros((128, 8), f32)
    consts[:, C_ONE] = 1.0
    consts[:, C_EPS] = 1e-5
    consts[:, C_75] = 7.5
    consts[:, C_KEPS] = 1e-24
    consts[:, C_EPS8] = 1e-8
    consts[:, C_HALF] = 0.5
    onesc = np.ones((128, 1), f32)
    onesr = np.ones((1, 128), f32)
    onescb = np.ones((128, 1), ml_dtypes.bfloat16)
    posr = np.arange(1, T + 1, dtype=f32).reshape(1, T)
    invf = (1.0 / 10000.0 ** (np.arange(0, 1024, 2, dtype=f32) / 1024.0)).astype(f32)
    cossin = []
    for half in range(2):
        pos = (half * T + np.arange(T, dtype=f32))
        fr = pos[None, :] * invf[:, None]
        cossin.append((np.cos(fr).astype(f32), np.sin(fr).astype(f32)))

    x0 = [_cc(X[c // 2, (c % 2) * T:(c % 2) * T + T, :].T[_P]) for c in cores]
    szero = [np.full((64, 1), float(c % 2), f32) for c in cores]

    nc = _nc("fused", build_fused)
    magics = np.array([[MAGIC0, MAGIC1, MAGIC2]], np.int32)
    in_maps = []
    for c in cores:
        in_maps.append(dict(
            x0=x0[c], consts=consts, onesc=onesc, onesr=onesr, onescb=onescb,
            winT=winT, binc=binc, cosr=cossin[c % 2][0], sinr=cossin[c % 2][1],
            woutT=woutT, boutc=boutc, gpk=gpk, gsc=gsc, upk=upk, usc=usc,
            dpk=dpk, dsc=dsc, wshT=wshT, gatec=gatec, lngc=lngc, lnbc=lnbc,
            wkvT=wkvT, tdc=tdc, tfc=tfc, tdrow=tdrow, posr=posr,
            w1T=w1T, w2T=w2T, w3T=w3T, woutpT=woutpT, szero=szero[c],
            halfI=np.array([[c % 2]], np.int32), magics=magics))
    res = run_bass_kernel_spmd(nc, in_maps, core_ids=cores)
    y = [res.results[c]["y"] for c in cores]
    out = np.empty((4, 4096, D), f32)
    for c in cores:
        out[c // 2, (c % 2) * T:(c % 2) * T + T, :] = y[c][_PINV].T
    return out
